# version 02
# baseline (speedup 1.0000x reference)
import math
from contextlib import ExitStack

import numpy as np
import ml_dtypes

import concourse.bass as bass
import concourse.mybir as mybir
from concourse.bass_utils import run_bass_kernel_spmd

F32 = mybir.dt.float32
BF16 = mybir.dt.bfloat16
AF = mybir.ActivationFunctionType
ALU = mybir.AluOpType
AX = mybir.AxisListType.X
BF = ml_dtypes.bfloat16

T = 4096
E = 4096
DH = 128
NH = 16
NKV0 = 4
HALF = 2048
HWIN = 256
NCORES = 8


def _rope0_tables():
    ROPE_THETA = 500000.0
    ROPE_FACTOR = 32.0
    HI, LO = 4.0, 1.0
    ORIG = 8192
    inv = 1.0 / ROPE_THETA ** (np.arange(0, DH, 2, dtype=np.float64) / DH)
    wl = 2.0 * math.pi / inv
    inv2 = np.where(wl > ORIG / LO, inv / ROPE_FACTOR, inv)
    smooth = (ORIG / wl - LO) / (HI - LO)
    smoothed = (1.0 - smooth) * inv2 / ROPE_FACTOR + smooth * inv2
    is_med = (~(wl < ORIG / LO)) & (~(wl > ORIG / HI))
    inv3 = np.where(is_med, smoothed, inv2)
    pos = np.arange(T, dtype=np.float64)
    f = pos[:, None] * inv3[None, :]
    emb = np.concatenate([f, f], axis=1)
    return np.cos(emb).T.astype(np.float32), np.sin(emb).T.astype(np.float32)


def _rope1_tables():
    base = 10000.0
    inv = 1.0 / base ** (np.arange(0, DH, 2, dtype=np.float64) / DH)
    t = np.arange(T, dtype=np.float64)
    f = np.outer(t, inv)
    cos = np.zeros((T, DH))
    sin = np.zeros((T, DH))
    cos[:, 0::2] = np.cos(f)
    cos[:, 1::2] = np.cos(f)
    sin[:, 0::2] = np.sin(f)
    sin[:, 1::2] = np.sin(f)
    return cos.T.astype(np.float32), sin.T.astype(np.float32)


def _host_tables():
    out = {}
    s = 1.0 / math.sqrt(DH)
    for lid, (cos, sin) in enumerate([_rope0_tables(), _rope1_tables()]):
        sgn = np.ones((DH, 1), np.float32)
        sgn[:64] = -1.0
        sinS = sin * sgn
        out[f"rp{lid}cq"] = (cos * s).astype(BF)
        out[f"rp{lid}sq"] = (sinS * s).astype(BF)
        out[f"rp{lid}ck"] = cos.astype(BF)
        out[f"rp{lid}sk"] = sinS.astype(BF)
    out["ident128"] = np.eye(128, dtype=BF)
    out["ident8"] = np.eye(8, dtype=BF)
    out["ones_col"] = np.ones((128, 1), dtype=BF)
    out["ones_row_f32"] = np.ones((1, 128), np.float32)
    out["ones_row_bf"] = np.ones((1, 128), dtype=BF)
    cm = np.zeros((128, 4 * 512), np.float32)
    r = np.arange(128)[:, None]
    c = np.arange(512)[None, :]
    for qt in range(4):
        cm[:, qt * 512:(qt + 1) * 512] = np.where(c <= qt * 128 + r, 0.0, -1e30)
    out["cmask"] = cm
    al = np.linspace(0.0, 1.0, HWIN, dtype=np.float32)[None, :]
    out["alpha_row"] = np.broadcast_to(al, (128, HWIN)).copy()
    return out


_CONST_SPECS = {
    "rp0cq": ([DH, T], BF16), "rp0sq": ([DH, T], BF16),
    "rp0ck": ([DH, T], BF16), "rp0sk": ([DH, T], BF16),
    "rp1cq": ([DH, T], BF16), "rp1sq": ([DH, T], BF16),
    "rp1ck": ([DH, T], BF16), "rp1sk": ([DH, T], BF16),
    "ident128": ([128, 128], BF16),
    "ident8": ([8, 8], BF16),
    "ones_col": ([128, 1], BF16),
    "ones_row_f32": ([1, 128], F32),
    "ones_row_bf": ([1, 128], BF16),
    "cmask": ([128, 2048], F32),
    "alpha_row": ([128, HWIN], F32),
}


class KB:
    def __init__(self):
        self.nc = bass.Bass("TRN2", target_bir_lowering=False, debug=False)
        self.es = ExitStack()
        self.sems = {}
        self.block = None
        self.consts = {}

    def sem(self, name):
        h = self.es.enter_context(self.nc.semaphore(name))
        self.sems[name] = [h, 0]
        return h

    def inc(self, instr, name, k=1):
        s = self.sems[name]
        instr.then_inc(s[0], k)
        s[1] += k
        return instr

    def n(self, name):
        return self.sems[name][1]

    def wge(self, eng, name, val=None):
        s = self.sems[name]
        v = s[1] if val is None else val
        if v > 0:
            eng.wait_ge(s[0], v)

    def barrier(self):
        targets = [(s[0], s[1]) for s in self.sems.values() if s[1] > 0]
        for engf in (self.block.sync, self.block.tensor, self.block.scalar,
                     self.block.vector, self.block.gpsimd):
            @engf
            def _(eng, targets=targets):
                for h, v in targets:
                    eng.wait_ge(h, v)

    def sb(self, name, shape, dtype, es=None):
        return (es or self.es).enter_context(self.nc.sbuf_tensor(name, shape, dtype))

    def ps(self, name, shape, dtype=F32, es=None):
        return (es or self.es).enter_context(self.nc.psum_tensor(name, shape, dtype))

    def load_consts(self, names):
        aps = {}
        for name in names:
            shape, dt = _CONST_SPECS[name]
            aps[name] = self.nc.dram_tensor(name, shape, dt, kind="ExternalInput").ap()
            self.consts[name] = self.sb("c_" + name, shape, dt)

        @self.block.sync
        def _(sync):
            for name in aps:
                self.inc(sync.dma_start(out=self.consts[name][:], in_=aps[name]), "ld", 16)


def _cast(b, dst, src):
    @b.block.gpsimd
    def _(gpsimd):
        b.inc(gpsimd.dma_start(out=dst, in_=src), "cast", 16)


def emit_ln_transpose(b, src_aps, R, xnT, tag, x1_out=None):
    nsrc = len(src_aps)
    assert nsrc <= 4
    with ExitStack() as pes:
        A = b.sb(f"{tag}A", [128, E], F32, pes)
        A2 = b.sb(f"{tag}A2", [128, E], F32, pes)
        A3 = b.sb(f"{tag}A3", [128, E], F32, pes)
        Bc = b.sb(f"{tag}B", [128, E], F32, pes)
        XN = [b.sb(f"{tag}xn{j}", [128, E], BF16, pes) for j in range(4)]
        ST = [b.sb(f"{tag}st{j}", [128, 512], BF16, pes) for j in range(2)]
        red = b.sb(f"{tag}red", [128, 8], F32, pes)
        ident = b.consts["ident128"]
        pT = [b.ps(f"{tag}pT{j}", [128, 512], F32, es=pes) for j in range(2)]

        for s in range(R // 512):
            for j in range(4):
                tt = s * 4 + j

                @b.block.sync
                def _(sync, tt=tt):
                    b.wge(sync, "dve")
                    b.wge(sync, "act")
                    b.wge(sync, "st")
                    b.wge(sync, "pe")
                    b.inc(sync.dma_start(out=A[:], in_=src_aps[0][tt * 128:(tt + 1) * 128, :]), "ld", 16)
                    for k in range(1, nsrc):
                        buf = [None, A2, A3, Bc][k]
                        b.inc(sync.dma_start(out=buf[:],
                                             in_=src_aps[k][tt * 128:(tt + 1) * 128, :]), "ld", 16)

                @b.block.vector
                def _(vector):
                    b.wge(vector, "ld")
                    for k in range(1, nsrc):
                        buf = [None, A2, A3, Bc][k]
                        b.inc(vector.tensor_add(A[:], A[:], buf[:]), "dve")

                if x1_out is not None:
                    @b.block.gpsimd
                    def _(gpsimd, tt=tt):
                        b.wge(gpsimd, "dve")
                        b.wge(gpsimd, "ld")
                        b.inc(gpsimd.dma_start(out=x1_out[tt * 128:(tt + 1) * 128, :], in_=A[:]), "st", 16)

                @b.block.vector
                def _(vector):
                    b.wge(vector, "ld")
                    b.inc(vector.reduce_sum(red[:, 0:1], A[:], axis=AX), "dve")
                    b.wge(vector, "dve")
                    b.inc(vector.tensor_scalar_mul(red[:, 1:2], red[:, 0:1], 1.0 / E), "dve")
                    b.wge(vector, "dve")
                    b.inc(vector.tensor_scalar(out=Bc[:], in0=A[:], scalar1=red[:, 1:2],
                                               scalar2=None, op0=ALU.subtract), "dve")

                @b.block.scalar
                def _(scalar):
                    b.wge(scalar, "dve")
                    b.inc(scalar.activation(out=A[:], in_=Bc[:], func=AF.Square,
                                            accum_out=red[:, 2:3]), "act")

                @b.block.vector
                def _(vector):
                    b.wge(vector, "act")
                    b.inc(vector.tensor_scalar(out=red[:, 3:4], in0=red[:, 2:3],
                                               scalar1=1.0 / E, scalar2=1e-5,
                                               op0=ALU.mult, op1=ALU.add), "dve")

                @b.block.scalar
                def _(scalar):
                    b.wge(scalar, "dve")
                    b.inc(scalar.activation(out=red[:, 4:5], in_=red[:, 3:4], func=AF.Sqrt), "act")

                @b.block.vector
                def _(vector, j=j):
                    b.wge(vector, "act")
                    b.inc(vector.reciprocal(red[:, 5:6], red[:, 4:5]), "dve")
                    b.wge(vector, "dve")
                    b.inc(vector.tensor_scalar(out=XN[j][:], in0=Bc[:], scalar1=red[:, 5:6],
                                               scalar2=None, op0=ALU.mult), "dve")

            for kc in range(32):
                pb = pT[kc % 2]
                stb = ST[kc % 2]

                @b.block.tensor
                def _(tensor, pb=pb):
                    b.wge(tensor, "dve")
                    b.wge(tensor, "act")
                    ins = None
                    for j in range(4):
                        ins = tensor.matmul(pb[:, j * 128:(j + 1) * 128],
                                            XN[j][:, kc * 128:(kc + 1) * 128],
                                            ident[:], start=True, stop=True)
                    b.inc(ins, "pe")

                @b.block.scalar
                def _(scalar, pb=pb, stb=stb):
                    b.wge(scalar, "pe")
                    b.wge(scalar, "st")
                    b.inc(scalar.activation(out=stb[:], in_=pb[:], func=AF.Copy), "act")

                @b.block.gpsimd
                def _(gpsimd, s=s, kc=kc, stb=stb):
                    b.wge(gpsimd, "act")
                    b.inc(gpsimd.dma_start(
                        out=xnT[kc * 128:(kc + 1) * 128, s * 512:(s + 1) * 512],
                        in_=stb[:]), "st", 16)
        b.barrier()


def emit_qkv(b, lid, xnT, wparts):
    with ExitStack() as pes:
        slab = b.sb(f"qs{lid}", [128, 32 * 512], BF16, pes)
        wbuf = b.sb(f"qw{lid}", [128, 32 * 512], BF16, pes)
        qh = b.sb(f"qh{lid}", [128, 512], BF16, pes)
        rot = b.sb(f"qr{lid}", [128, 512], BF16, pes)
        t1 = b.sb(f"qt1{lid}", [128, 512], BF16, pes)
        pP = b.ps(f"qp{lid}", [128, 512], F32, es=pes)

        for tg in range(T // 512):
            t0 = tg * 512

            @b.block.sync
            def _(sync, t0=t0):
                b.wge(sync, "pe")
                for kc in range(32):
                    b.inc(sync.dma_start(out=slab[:, kc * 512:(kc + 1) * 512],
                                         in_=xnT[kc * 128:(kc + 1) * 128, t0:t0 + 512]), "ld", 16)

            for part in wparts:
                rows = part["rows"]
                GW = min(512, rows)
                for ng in range(rows // GW):
                    @b.block.sync
                    def _(sync, part=part, ng=ng, GW=GW):
                        b.wge(sync, "pe")
                        for kc in range(32):
                            b.inc(sync.dma_start_transpose(
                                out=wbuf[:, kc * GW:kc * GW + GW],
                                in_=part["wB"][ng * GW:(ng + 1) * GW,
                                               kc * 128:(kc + 1) * 128]), "ld", 16)

                    for sub in range(GW // 128 if part["fmajor"] else 4):
                        if part["fmajor"]:
                            @b.block.tensor
                            def _(tensor, sub=sub):
                                b.wge(tensor, "ld")
                                b.wge(tensor, "act")
                                ins = None
                                for kc in range(32):
                                    ins = tensor.matmul(
                                        pP[:],
                                        wbuf[:, kc * GW + sub * 128:kc * GW + (sub + 1) * 128],
                                        slab[:, kc * 512:(kc + 1) * 512],
                                        start=(kc == 0), stop=(kc == 31))
                                b.inc(ins, "pe")
                        else:
                            @b.block.tensor
                            def _(tensor, sub=sub):
                                b.wge(tensor, "ld")
                                b.wge(tensor, "act")
                                ins = None
                                for kc in range(32):
                                    ins = tensor.matmul(
                                        pP[:, 0:GW],
                                        slab[:, kc * 512 + sub * 128:kc * 512 + (sub + 1) * 128],
                                        wbuf[:, kc * GW:kc * GW + GW],
                                        start=(kc == 0), stop=(kc == 31))
                                b.inc(ins, "pe")

                        @b.block.scalar
                        def _(scalar):
                            b.wge(scalar, "pe")
                            b.wge(scalar, "st")
                            b.wge(scalar, "dve")
                            if part["fmajor"]:
                                b.inc(scalar.activation(out=qh[:], in_=pP[:], func=AF.Copy), "act")
                            else:
                                b.inc(scalar.activation(out=qh[:, 0:GW], in_=pP[:, 0:GW], func=AF.Copy), "act")

                        if part["rope"] is not None:
                            cs = b.consts[part["rope"][0]]
                            sn = b.consts[part["rope"][1]]

                            @b.block.vector
                            def _(vector, t0=t0):
                                b.wge(vector, "act")
                                b.inc(vector.tensor_copy(rot[0:64, :], qh[64:128, :]), "dve")
                                b.inc(vector.tensor_copy(rot[64:128, :], qh[0:64, :]), "dve")
                                b.inc(vector.tensor_mul(t1[:], qh[:], cs[:, t0:t0 + 512]), "dve")
                                b.inc(vector.tensor_mul(rot[:], rot[:], sn[:, t0:t0 + 512]), "dve")
                                b.inc(vector.tensor_add(qh[:], t1[:], rot[:]), "dve")
                            wsem = "dve"
                        else:
                            wsem = "act"

                        @b.block.gpsimd
                        def _(gpsimd, part=part, ng=ng, sub=sub, t0=t0, wsem=wsem):
                            b.wge(gpsimd, wsem)
                            if part["fmajor"]:
                                n0 = ng * GW + sub * 128
                                dst = part["out"][n0:n0 + 128, t0:t0 + 512]
                                src_ap = qh[:]
                            else:
                                dst = part["out"][t0 + sub * 128:t0 + (sub + 1) * 128,
                                                  ng * GW:(ng + 1) * GW]
                                src_ap = qh[:, 0:GW]
                            b.inc(gpsimd.dma_start(out=dst, in_=src_ap), "st", 16)
        b.barrier()


def emit_attn_l0(b, QT, KT, V, CT):
    ident = b.consts["ident128"]
    ones_col = b.consts["ones_col"]
    ones_row = b.consts["ones_row_f32"]
    cmask = b.consts["cmask"]
    with ExitStack() as pes:
        Kg = b.sb("aKg", [128, T], BF16, pes)
        Vg = b.sb("aVg", [128, 32 * 128], BF16, pes)
        Qh = b.sb("aQh", [128, T], BF16, pes)
        sq = b.sb("asq", [128, 512], BF16, pes)
        P = [b.sb(f"aP{qt}", [128, T], BF16, pes) for qt in range(4)]
        PTs = b.sb("aPT", [128, 512], BF16, pes)
        sums = b.sb("asums", [128, 4 * 8], F32, pes)
        rcp = b.sb("arcp", [128, 4], F32, pes)
        diag = b.sb("adiag", [128, 4 * 128], BF16, pes)
        biasB = b.sb("abias", [128, 1], F32, pes)
        scal = b.sb("ascal", [1, 8], F32, pes)
        cstg = b.sb("acst", [128, 512], BF16, pes)
        pS = [b.ps(f"apS{qt}", [128, 512], F32, es=pes) for qt in range(4)]
        pPT = b.ps("apPT", [128, 512], F32, es=pes)
        pCtx = b.ps("apCtx", [128, 512], F32, es=pes)
        pAux = b.ps("apAux", [128, 512], F32, es=pes)

        def colmax2(src_sbuf, ncols, out_slot):
            for c in range(ncols // 512):
                @b.block.scalar
                def _(scalar, c=c):
                    b.wge(scalar, "pe")
                    b.wge(scalar, "dve")
                    b.inc(scalar.activation(out=sq[:], in_=src_sbuf[:, c * 512:(c + 1) * 512],
                                            func=AF.Square), "act")

                @b.block.tensor
                def _(tensor):
                    b.wge(tensor, "act")
                    b.wge(tensor, "dve")
                    b.inc(tensor.matmul(pAux[0:1, :], ones_col[:], sq[:],
                                        start=True, stop=True), "pe")

                @b.block.vector
                def _(vector, c=c):
                    b.wge(vector, "pe")
                    b.inc(vector.reduce_max(scal[0:1, 6:7], pAux[0:1, :], axis=AX), "dve")
                    b.wge(vector, "dve")
                    if c == 0:
                        b.inc(vector.tensor_copy(scal[0:1, out_slot:out_slot + 1],
                                                 scal[0:1, 6:7]), "dve")
                    else:
                        b.inc(vector.tensor_tensor(scal[0:1, out_slot:out_slot + 1],
                                                   scal[0:1, out_slot:out_slot + 1],
                                                   scal[0:1, 6:7], op=ALU.max), "dve")

        for g in range(NKV0):
            @b.block.sync
            def _(sync, g=g):
                b.wge(sync, "pe")
                b.wge(sync, "act")
                b.inc(sync.dma_start(out=Kg[:], in_=KT[g * 128:(g + 1) * 128, :]), "ld", 16)
                for kgc in range(32):
                    b.inc(sync.dma_start(out=Vg[:, kgc * 128:(kgc + 1) * 128],
                                         in_=V[kgc * 128:(kgc + 1) * 128,
                                               g * 128:(g + 1) * 128]), "ld", 16)
            b.barrier()
            colmax2(Kg, T, 1)
            b.barrier()

            for u in range(4):
                h = g * 4 + u

                @b.block.sync
                def _(sync, h=h):
                    b.wge(sync, "pe")
                    b.wge(sync, "act")
                    b.inc(sync.dma_start(out=Qh[:], in_=QT[h * 128:(h + 1) * 128, :]), "ld", 16)
                b.barrier()
                colmax2(Qh, T, 0)

                @b.block.vector
                def _(vector):
                    b.wge(vector, "dve")
                    b.inc(vector.tensor_mul(scal[0:1, 2:3], scal[0:1, 0:1], scal[0:1, 1:2]), "dve")

                @b.block.scalar
                def _(scalar):
                    b.wge(scalar, "dve")
                    b.inc(scalar.activation(out=scal[0:1, 3:4], in_=scal[0:1, 2:3],
                                            func=AF.Sqrt), "act")

                @b.block.vector
                def _(vector):
                    b.wge(vector, "act")
                    b.inc(vector.tensor_scalar_mul(scal[0:1, 4:5], scal[0:1, 3:4], -1.0), "dve")

                @b.block.tensor
                def _(tensor):
                    b.wge(tensor, "dve")
                    b.inc(tensor.matmul(pAux[:, 0:1], ones_row[:], scal[0:1, 4:5],
                                        start=True, stop=True), "pe")

                @b.block.vector
                def _(vector):
                    b.wge(vector, "pe")
                    b.inc(vector.tensor_copy(biasB[:], pAux[:, 0:1]), "dve")
                b.barrier()

                for qg in range(8):
                    nkg = qg + 1
                    for kg in range(nkg):
                        for qt in range(4):
                            @b.block.tensor
                            def _(tensor, qg=qg, kg=kg, qt=qt):
                                b.wge(tensor, "act")
                                b.inc(tensor.matmul(
                                    pS[qt][:],
                                    Qh[:, qg * 512 + qt * 128:qg * 512 + (qt + 1) * 128],
                                    Kg[:, kg * 512:(kg + 1) * 512],
                                    start=True, stop=True), "pe")

                        if kg == qg:
                            @b.block.vector
                            def _(vector):
                                b.wge(vector, "pe")
                                for qt in range(4):
                                    b.inc(vector.tensor_add(pS[qt][:], pS[qt][:],
                                                            cmask[:, qt * 512:(qt + 1) * 512]), "dve")

                        for qt in range(4):
                            @b.block.scalar
                            def _(scalar, kg=kg, qt=qt, qg=qg):
                                b.wge(scalar, "pe")
                                if kg == qg:
                                    b.wge(scalar, "dve")
                                b.inc(scalar.activation(
                                    out=P[qt][:, kg * 512:(kg + 1) * 512],
                                    in_=pS[qt][:], func=AF.Exp, bias=biasB[:, 0:1],
                                    accum_out=sums[:, qt * 8 + kg:qt * 8 + kg + 1]), "act")

                    @b.block.vector
                    def _(vector, nkg=nkg):
                        b.wge(vector, "act")
                        b.wge(vector, "pe")
                        for qt in range(4):
                            b.inc(vector.reduce_sum(rcp[:, qt:qt + 1],
                                                    sums[:, qt * 8:qt * 8 + nkg], axis=AX), "dve")
                        b.wge(vector, "dve")
                        for qt in range(4):
                            b.inc(vector.reciprocal(rcp[:, qt:qt + 1], rcp[:, qt:qt + 1]), "dve")
                        b.wge(vector, "dve")
                        for qt in range(4):
                            b.inc(vector.tensor_scalar(
                                out=diag[:, qt * 128:(qt + 1) * 128], in0=ident[:],
                                scalar1=rcp[:, qt:qt + 1], scalar2=None, op0=ALU.mult), "dve")

                    npv = 0
                    for kg in range(nkg):
                        for tkc in range(4):
                            @b.block.tensor
                            def _(tensor, kg=kg, tkc=tkc):
                                b.wge(tensor, "dve")
                                b.wge(tensor, "act")
                                ins = None
                                for qt in range(4):
                                    ins = tensor.matmul(
                                        pPT[:, qt * 128:(qt + 1) * 128],
                                        P[qt][:, kg * 512 + tkc * 128:kg * 512 + (tkc + 1) * 128],
                                        diag[:, qt * 128:(qt + 1) * 128],
                                        start=True, stop=True)
                                b.inc(ins, "pe")

                            @b.block.scalar
                            def _(scalar):
                                b.wge(scalar, "pe")
                                b.inc(scalar.activation(out=PTs[:], in_=pPT[:], func=AF.Copy), "act")

                            npv += 1

                            @b.block.tensor
                            def _(tensor, kg=kg, tkc=tkc, npv=npv, nkg=nkg):
                                b.wge(tensor, "act")
                                b.inc(tensor.matmul(
                                    pCtx[:], Vg[:, (kg * 4 + tkc) * 128:(kg * 4 + tkc + 1) * 128],
                                    PTs[:], start=(npv == 1), stop=(npv == 4 * nkg),
                                    skip_group_check=True), "pe")

                    @b.block.scalar
                    def _(scalar):
                        b.wge(scalar, "pe")
                        b.wge(scalar, "st")
                        b.inc(scalar.activation(out=cstg[:], in_=pCtx[:], func=AF.Copy), "act")

                    @b.block.gpsimd
                    def _(gpsimd, h=h, qg=qg):
                        b.wge(gpsimd, "act")
                        b.inc(gpsimd.dma_start(
                            out=CT[h * 128:(h + 1) * 128, qg * 512:(qg + 1) * 512],
                            in_=cstg[:]), "st", 16)
                b.barrier()


def emit_dense_tok(b, actT, K, wB, out, Rtok, tag, slabT=1024, bias_sb=None):
    nkc = K // 128
    with ExitStack() as pes:
        slab = b.sb(f"{tag}sl", [128, nkc * slabT], BF16, pes)
        wbuf = b.sb(f"{tag}wb", [128, nkc * 512], BF16, pes)
        stg = [b.sb(f"{tag}sg{j}", [128, 512], F32, pes) for j in range(2)]
        pD = [b.ps(f"{tag}pD{j}", [128, 512], F32, es=pes) for j in range(2)]

        for sl in range(Rtok // slabT):
            tbase = sl * slabT

            @b.block.sync
            def _(sync, tbase=tbase):
                b.wge(sync, "pe")
                for kc in range(nkc):
                    b.inc(sync.dma_start(out=slab[:, kc * slabT:(kc + 1) * slabT],
                                         in_=actT[kc * 128:(kc + 1) * 128,
                                                  tbase:tbase + slabT]), "ld", 16)

            for eg in range(8):
                @b.block.sync
                def _(sync, eg=eg):
                    b.wge(sync, "pe")
                    for kc in range(nkc):
                        b.inc(sync.dma_start_transpose(
                            out=wbuf[:, kc * 512:(kc + 1) * 512],
                            in_=wB[eg * 512:(eg + 1) * 512, kc * 128:(kc + 1) * 128]), "ld", 16)

                for tt in range(slabT // 128):
                    pb = pD[tt % 2]
                    sg = stg[tt % 2]

                    @b.block.tensor
                    def _(tensor, tt=tt, pb=pb, eg=eg):
                        b.wge(tensor, "ld")
                        b.wge(tensor, "act")
                        ins = None
                        for kc in range(nkc):
                            ins = tensor.matmul(
                                pb[:],
                                slab[:, kc * slabT + tt * 128:kc * slabT + (tt + 1) * 128],
                                wbuf[:, kc * 512:(kc + 1) * 512],
                                start=(kc == 0),
                                stop=(kc == nkc - 1 and bias_sb is None))
                        if bias_sb is not None:
                            ins = tensor.matmul(
                                pb[:], b.consts["ones_row_bf"][:],
                                bias_sb[0:1, eg * 512:(eg + 1) * 512],
                                start=False, stop=True, skip_group_check=True)
                        b.inc(ins, "pe")

                    @b.block.scalar
                    def _(scalar, pb=pb, sg=sg):
                        b.wge(scalar, "pe")
                        b.wge(scalar, "st")
                        b.inc(scalar.activation(out=sg[:], in_=pb[:], func=AF.Copy), "act")

                    @b.block.gpsimd
                    def _(gpsimd, tbase=tbase, tt=tt, eg=eg, sg=sg):
                        b.wge(gpsimd, "act")
                        b.inc(gpsimd.dma_start(
                            out=out[tbase + tt * 128:tbase + (tt + 1) * 128,
                                    eg * 512:(eg + 1) * 512],
                            in_=sg[:]), "st", 16)
        b.barrier()


def emit_ffn_w_build(b, fw, fws, fwb, fwB):
    with ExitStack() as pes:
        A = b.sb("fbA", [128, E], F32, pes)
        S = b.sb("fbS", [128, E], BF16, pes)
        W = b.sb("fbW", [128, E], BF16, pes)
        sc = b.sb("fbsc", [128, 2], F32, pes)

        for rb in range(32):
            @b.block.sync
            def _(sync, rb=rb):
                b.wge(sync, "act")
                b.wge(sync, "dve")
                b.wge(sync, "st")
                b.inc(sync.dma_start(out=A[:], in_=fw[rb * 128:(rb + 1) * 128, :]), "ld", 16)
                b.inc(sync.dma_start(out=sc[:, 0:1], in_=fws[rb * 128:(rb + 1) * 128, :]), "ld", 16)
                b.inc(sync.dma_start(out=sc[:, 1:2], in_=fwb[rb * 128:(rb + 1) * 128, :]), "ld", 16)

            @b.block.scalar
            def _(scalar):
                b.wge(scalar, "ld")
                b.wge(scalar, "dve")
                b.inc(scalar.activation(out=S[:], in_=A[:], func=AF.Sign), "act")

            @b.block.vector
            def _(vector):
                b.wge(vector, "act")
                b.wge(vector, "st")
                b.inc(vector.tensor_scalar(out=W[:], in0=S[:], scalar1=sc[:, 0:1],
                                           scalar2=sc[:, 1:2], op0=ALU.mult,
                                           op1=ALU.add), "dve")

            @b.block.gpsimd
            def _(gpsimd, rb=rb):
                b.wge(gpsimd, "dve")
                b.inc(gpsimd.dma_start(out=fwB[rb * 128:(rb + 1) * 128, :], in_=W[:]), "st", 16)
        b.barrier()


def build_launch1(dbg=()):
    b = KB()
    nc = b.nc

    def dram(name, shape, dtype, kind="Internal"):
        kind2 = "ExternalOutput" if (name in dbg and kind == "Internal") else kind
        return nc.dram_tensor(name, shape, dtype, kind=kind2).ap()

    x_full = dram("x_full", [T, E], F32, "ExternalInput")
    x_own = dram("x_own", [HALF, E], F32, "ExternalInput")
    wq = dram("wq", [NH * DH, E], F32, "ExternalInput")
    wk = dram("wk", [NKV0 * DH, E], F32, "ExternalInput")
    wv = dram("wv", [NKV0 * DH, E], F32, "ExternalInput")
    wo = dram("wo", [E, NH * DH], F32, "ExternalInput")
    fw = dram("fw", [E, E], F32, "ExternalInput")
    fws = dram("fws", [E, 1], F32, "ExternalInput")
    fwb = dram("fwb", [E, 1], F32, "ExternalInput")

    S_h = dram("S_h", [T, E], F32, "ExternalOutput")
    Zf = dram("Zf", [HALF, E], F32, "ExternalOutput")

    xnT = dram("xnT", [E, T], BF16)
    xnTo = dram("xnTo", [E, HALF], BF16)
    wqB = dram("wqB", [NH * DH, E], BF16)
    wkB = dram("wkB", [NKV0 * DH, E], BF16)
    wvB = dram("wvB", [NKV0 * DH, E], BF16)
    woB = dram("woB", [E, NH * DH], BF16)
    fwB = dram("fwB", [E, E], BF16)
    QT = dram("QT", [NH * DH, T], BF16)
    KT = dram("KT", [NKV0 * DH, T], BF16)
    V = dram("V", [T, NKV0 * DH], BF16)
    CT = dram("CT", [NH * DH, T], BF16)

    b.block = b.es.enter_context(nc.Block())
    for s in ("ld", "pe", "act", "dve", "st", "cast", "pool"):
        b.sem(s)

    with b.es:
        b.load_consts(["ident128", "ones_col", "ones_row_f32", "cmask",
                       "rp0cq", "rp0sq", "rp0ck", "rp0sk"])
        _cast(b, wqB, wq)
        _cast(b, wkB, wk)
        _cast(b, wvB, wv)
        _cast(b, woB, wo)

        emit_ln_transpose(b, [x_full], T, xnT, "p0")
        emit_ln_transpose(b, [x_own], HALF, xnTo, "p0o")

        @b.block.sync
        def _(sync):
            b.wge(sync, "cast")
        b.barrier()

        emit_qkv(b, 0, xnT, [
            {"rows": NH * DH, "wB": wqB, "out": QT, "fmajor": True,
             "rope": ("rp0cq", "rp0sq")},
            {"rows": NKV0 * DH, "wB": wkB, "out": KT, "fmajor": True,
             "rope": ("rp0ck", "rp0sk")},
            {"rows": NKV0 * DH, "wB": wvB, "out": V, "fmajor": False,
             "rope": None},
        ])

        emit_attn_l0(b, QT, KT, V, CT)

        emit_dense_tok(b, CT, NH * DH, woB, S_h, T, "wo", slabT=2048)

        emit_ffn_w_build(b, fw, fws, fwb, fwB)
        emit_dense_tok(b, xnTo, E, fwB, Zf, HALF, "ffn", slabT=1024)

        @b.block.sync
        def _(sync):
            b.wge(sync, "st")
            b.wge(sync, "cast")
    return b.nc


def emit_gate_pre(b, xn1T, A_bf, W8bf, bvecB, limesB, zT, Hs, ds, pmax):
    ident8 = b.consts["ident8"]
    ones_rb = b.consts["ones_row_bf"]
    with ExitStack() as pes:
        slab = b.sb("gs", [128, 32 * 512], BF16, pes)
        asb = b.sb("gasb", [128, 32 * 32], BF16, pes)
        w8 = b.sb("gw8", [8, E], BF16, pes)
        w8T = b.sb("gw8T", [128, 32 * 8], BF16, pes)
        lim = b.sb("glim", [8, 32], BF16, pes)
        limT = b.sb("glimT", [32, 8], BF16, pes)
        b8 = b.sb("gb8", [1, 8], BF16, pes)
        zsb = b.sb("gz", [32, 512], BF16, pes)
        aslab = b.sb("gaslab", [128, 32 * 256], BF16, pes)
        za = b.sb("gza", [32, 256], BF16, pes)
        hs_sb = b.sb("ghs", [128, 8], F32, pes)
        ds_sb = b.sb("gds", [128, 8], F32, pes)
        pm = b.sb("gpm", [128, 4], F32, pes)
        stg = b.sb("gstg", [128, 8], BF16, pes)
        pZ = b.ps("gpZ", [128, 512], F32, es=pes)
        pH = b.ps("gpH", [128, 512], F32, es=pes)

        @b.block.sync
        def _(sync):
            for kc in range(32):
                b.inc(sync.dma_start(out=asb[:, kc * 32:(kc + 1) * 32],
                                     in_=A_bf[kc * 128:(kc + 1) * 128, :]), "ld", 16)
            b.inc(sync.dma_start(out=w8[:], in_=W8bf), "ld", 16)
            b.inc(sync.dma_start(out=lim[:], in_=limesB), "ld", 16)
            b.inc(sync.dma_start(out=b8[:], in_=bvecB[0:1, 0:8]), "ld", 16)

        @b.block.tensor
        def _(tensor):
            b.wge(tensor, "ld")
            for kc in range(32):
                b.inc(tensor.matmul(pH[:, kc * 8:(kc + 1) * 8] if kc < 64 else None,
                                    w8[:, kc * 128:(kc + 1) * 128], ident8[:],
                                    start=True, stop=True), "pe")
            b.inc(tensor.matmul(pZ[0:32, 0:8], lim[:], ident8[:],
                                start=True, stop=True), "pe")

        @b.block.vector
        def _(vector):
            b.wge(vector, "pe")
            b.inc(vector.tensor_copy(w8T[:], pH[:, 0:256]), "dve")
            b.inc(vector.tensor_copy(limT[:], pZ[0:32, 0:8]), "dve")
            b.inc(vector.memset(pm[:], 0.0), "dve")
        b.barrier()

        for tg in range(T // 512):
            t0 = tg * 512

            @b.block.sync
            def _(sync, t0=t0):
                b.wge(sync, "pe")
                for kc in range(32):
                    b.inc(sync.dma_start(out=slab[:, kc * 512:(kc + 1) * 512],
                                         in_=xn1T[kc * 128:(kc + 1) * 128, t0:t0 + 512]), "ld", 16)

            @b.block.vector
            def _(vector):
                b.wge(vector, "ld")
                b.wge(vector, "pe")
                ins = None
                for kc in range(32):
                    ins = vector.tensor_copy(aslab[:, kc * 256:(kc + 1) * 256],
                                             slab[:, kc * 512 + 1:kc * 512 + 512:2])
                b.inc(ins, "dve")

            @b.block.tensor
            def _(tensor):
                b.wge(tensor, "ld")
                b.wge(tensor, "act")
                ins = None
                for kc in range(32):
                    ins = tensor.matmul(pZ[0:32, :], asb[:, kc * 32:(kc + 1) * 32],
                                        slab[:, kc * 512:(kc + 1) * 512],
                                        start=(kc == 0), stop=(kc == 31))
                b.inc(ins, "pe")

            @b.block.scalar
            def _(scalar):
                b.wge(scalar, "pe")
                b.wge(scalar, "st")
                b.inc(scalar.activation(out=zsb[:], in_=pZ[0:32, :], func=AF.Copy), "act")

            @b.block.gpsimd
            def _(gpsimd, t0=t0):
                b.wge(gpsimd, "act")
                b.inc(gpsimd.dma_start(out=zT[0:32, t0:t0 + 512], in_=zsb[:]), "st", 16)

            @b.block.vector
            def _(vector):
                b.wge(vector, "act")
                b.wge(vector, "pe")
                b.inc(vector.tensor_copy(za[:], zsb[:, 1:512:2]), "dve")

            for at in range(2):
                a0 = t0 // 2 + at * 128
                c0 = at * 256 + 1

                @b.block.tensor
                def _(tensor, c0=c0):
                    b.wge(tensor, "act")
                    b.wge(tensor, "dve")
                    ins = None
                    for kc in range(32):
                        ins = tensor.matmul(
                            pH[:, 0:8],
                            aslab[:, kc * 256 + at * 128:kc * 256 + (at + 1) * 128],
                            w8T[:, kc * 8:(kc + 1) * 8],
                            start=(kc == 0), stop=False)
                    ins = tensor.matmul(pH[:, 0:8], ones_rb[:], b8[:],
                                        start=False, stop=True, skip_group_check=True)
                    b.inc(ins, "pe")
                    b.inc(tensor.matmul(pH[:, 8:16], za[:, at * 128:(at + 1) * 128],
                                        limT[:], start=True, stop=True), "pe")

                @b.block.vector
                def _(vector, a0=a0):
                    b.wge(vector, "pe")
                    b.wge(vector, "st")
                    b.inc(vector.tensor_copy(hs_sb[:], pH[:, 0:8]), "dve")
                    b.inc(vector.tensor_copy(ds_sb[:], pH[:, 8:16]), "dve")
                    b.wge(vector, "dve")
                    b.inc(vector.reduce_max(pm[:, 2:3], hs_sb[:], axis=AX,
                                            apply_absolute_value=True), "dve")
                    b.inc(vector.reduce_max(pm[:, 3:4], ds_sb[:], axis=AX,
                                            apply_absolute_value=True), "dve")
                    b.wge(vector, "dve")
                    b.inc(vector.tensor_tensor(pm[:, 0:1], pm[:, 0:1], pm[:, 2:3], op=ALU.max), "dve")
                    b.inc(vector.tensor_tensor(pm[:, 1:2], pm[:, 1:2], pm[:, 3:4], op=ALU.max), "dve")

                @b.block.gpsimd
                def _(gpsimd, a0=a0):
                    b.wge(gpsimd, "dve")
                    b.inc(gpsimd.dma_start(out=Hs[a0:a0 + 128, :], in_=hs_sb[:]), "st", 16)
                    b.inc(gpsimd.dma_start(out=ds[a0:a0 + 128, :], in_=ds_sb[:]), "st", 16)

        @b.block.gpsimd
        def _(gpsimd):
            b.wge(gpsimd, "dve")
            b.inc(gpsimd.dma_start(out=pmax, in_=pm[:, 0:2]), "st", 16)
        b.barrier()


def emit_attn_l1(b, QT1, KT1, V1, CT1):
    ident = b.consts["ident128"]
    alpha = b.consts["alpha_row"]
    with ExitStack() as pes:
        K1 = b.sb("wK1", [128, T], BF16, pes)
        Vg = b.sb("wVg", [128, 32 * 128], BF16, pes)
        Qh = b.sb("wQh", [128, T], BF16, pes)
        Pw = [b.sb(f"wP{qt}", [128, 512], BF16, pes) for qt in range(4)]
        PTs = b.sb("wPT", [128, 512], BF16, pes)
        rcp = b.sb("wrcp", [128, 4], F32, pes)
        sums = b.sb("wsums", [128, 4], F32, pes)
        diag = b.sb("wdiag", [128, 4 * 128], BF16, pes)
        cw = [b.sb(f"wcw{j}", [128, 512], F32, pes) for j in range(2)]
        blnd = b.sb("wbl", [128, HWIN], F32, pes)
        t1 = b.sb("wt1", [128, HWIN], F32, pes)
        cstg = b.sb("wcst", [128, HWIN], BF16, pes)
        pS = [b.ps(f"wpS{qt}", [128, 512], F32, es=pes) for qt in range(4)]
        pPT = b.ps("wpPT", [128, 512], F32, es=pes)
        pCtx = b.ps("wpCtx", [128, 512], F32, es=pes)

        @b.block.sync
        def _(sync):
            b.inc(sync.dma_start(out=K1[:], in_=KT1[0:128, :]), "ld", 16)
            for kgc in range(32):
                b.inc(sync.dma_start(out=Vg[:, kgc * 128:(kgc + 1) * 128],
                                     in_=V1[kgc * 128:(kgc + 1) * 128, 0:128]), "ld", 16)
        b.barrier()

        for h in range(NH):
            @b.block.sync
            def _(sync, h=h):
                b.wge(sync, "pe")
                b.wge(sync, "act")
                b.inc(sync.dma_start(out=Qh[:], in_=QT1[h * 128:(h + 1) * 128, :]), "ld", 16)
            b.barrier()

            for wi, i in enumerate(range(0, T, HWIN)):
                end = min(i + 512, T)
                nq = (end - i) // 128
                nk = end - i
                cur = cw[wi % 2]
                prev = cw[(wi + 1) % 2]

                for qt in range(nq):
                    @b.block.tensor
                    def _(tensor, i=i, qt=qt, nk=nk):
                        b.wge(tensor, "ld")
                        b.wge(tensor, "act")
                        b.inc(tensor.matmul(
                            pS[qt][:, 0:nk],
                            Qh[:, i + qt * 128:i + (qt + 1) * 128],
                            K1[:, i:i + nk], start=True, stop=True), "pe")

                    @b.block.scalar
                    def _(scalar, qt=qt, nk=nk):
                        b.wge(scalar, "pe")
                        b.wge(scalar, "dve")
                        b.inc(scalar.activation(
                            out=Pw[qt][:, 0:nk], in_=pS[qt][:, 0:nk], func=AF.Sigmoid,
                            accum_out=sums[:, qt:qt + 1]), "act")

                @b.block.vector
                def _(vector, nq=nq):
                    b.wge(vector, "act")
                    b.wge(vector, "pe")
                    for qt in range(nq):
                        b.inc(vector.reciprocal(rcp[:, qt:qt + 1], sums[:, qt:qt + 1]), "dve")
                    b.wge(vector, "dve")
                    for qt in range(nq):
                        b.inc(vector.tensor_scalar(
                            out=diag[:, qt * 128:(qt + 1) * 128], in0=ident[:],
                            scalar1=rcp[:, qt:qt + 1], scalar2=None, op0=ALU.mult), "dve")

                npv = 0
                for tkc in range(nk // 128):
                    @b.block.tensor
                    def _(tensor, tkc=tkc, nq=nq):
                        b.wge(tensor, "dve")
                        b.wge(tensor, "act")
                        ins = None
                        for qt in range(nq):
                            ins = tensor.matmul(
                                pPT[:, qt * 128:(qt + 1) * 128],
                                Pw[qt][:, tkc * 128:(tkc + 1) * 128],
                                diag[:, qt * 128:(qt + 1) * 128],
                                start=True, stop=True)
                        b.inc(ins, "pe")

                    @b.block.scalar
                    def _(scalar, nq=nq):
                        b.wge(scalar, "pe")
                        b.inc(scalar.activation(out=PTs[:, 0:nq * 128],
                                                in_=pPT[:, 0:nq * 128], func=AF.Copy), "act")

                    npv += 1

                    @b.block.tensor
                    def _(tensor, i=i, tkc=tkc, npv=npv, nk=nk, nq=nq):
                        b.wge(tensor, "act")
                        b.inc(tensor.matmul(
                            pCtx[:, 0:nq * 128],
                            Vg[:, (i // 128 + tkc) * 128:(i // 128 + tkc + 1) * 128],
                            PTs[:, 0:nq * 128], start=(npv == 1), stop=(npv == nk // 128),
                            skip_group_check=True), "pe")

                @b.block.vector
                def _(vector, cur=cur, nq=nq):
                    b.wge(vector, "pe")
                    b.inc(vector.tensor_copy(cur[:, 0:nq * 128], pCtx[:, 0:nq * 128]), "dve")

                @b.block.vector
                def _(vector, wi=wi, cur=cur, prev=prev):
                    b.wge(vector, "dve")
                    if wi == 0:
                        b.inc(vector.tensor_copy(blnd[:], cur[:, 0:HWIN]), "dve")
                    else:
                        b.inc(vector.tensor_sub(t1[:], cur[:, 0:HWIN], prev[:, HWIN:512]), "dve")
                        b.inc(vector.tensor_mul(t1[:], t1[:], alpha[:]), "dve")
                        b.inc(vector.tensor_add(blnd[:], t1[:], prev[:, HWIN:512]), "dve")

                @b.block.scalar
                def _(scalar):
                    b.wge(scalar, "dve")
                    b.wge(scalar, "st")
                    b.inc(scalar.activation(out=cstg[:], in_=blnd[:], func=AF.Copy), "act")

                @b.block.gpsimd
                def _(gpsimd, h=h, i=i):
                    b.wge(gpsimd, "act")
                    b.inc(gpsimd.dma_start(
                        out=CT1[h * 128:(h + 1) * 128, i:i + HWIN], in_=cstg[:]), "st", 16)
        b.barrier()


def build_launch2(dbg=()):
    b = KB()
    nc = b.nc

    def dram(name, shape, dtype, kind="Internal"):
        kind2 = "ExternalOutput" if (name in dbg and kind == "Internal") else kind
        return nc.dram_tensor(name, shape, dtype, kind=kind2).ap()

    x_full = dram("x_full", [T, E], F32, "ExternalInput")
    S0 = dram("S0", [T, E], F32, "ExternalInput")
    S1i = dram("S1i", [T, E], F32, "ExternalInput")
    Zfc = dram("Zfc", [T, E], F32, "ExternalInput")
    x_own = dram("x_own", [HALF, E], F32, "ExternalInput")
    S0o = dram("S0o", [HALF, E], F32, "ExternalInput")
    S1o = dram("S1o", [HALF, E], F32, "ExternalInput")
    Zfo = dram("Zfo", [HALF, E], F32, "ExternalInput")
    wq1 = dram("wq1", [NH * DH, E], F32, "ExternalInput")
    wk1 = dram("wk1", [DH, E], F32, "ExternalInput")
    wv1 = dram("wv1", [DH, E], F32, "ExternalInput")
    wo1 = dram("wo1", [E, NH * DH], F32, "ExternalInput")
    Wf = dram("Wf", [E, E], F32, "ExternalInput")
    bvec = dram("bvec", [1, E], F32, "ExternalInput")
    Amat = dram("Amat", [E, 32], F32, "ExternalInput")
    W8 = dram("W8", [8, E], F32, "ExternalInput")
    limes = dram("limes", [8, 32], F32, "ExternalInput")

    S1_h = dram("S1_h", [T, E], F32, "ExternalOutput")
    Hout = dram("Hout", [HALF, E], F32, "ExternalOutput")
    x1_own = dram("x1_own", [HALF, E], F32, "ExternalOutput")
    zT = dram("zT", [32, T], BF16, "ExternalOutput")
    Hs = dram("Hs", [HALF, 8], F32, "ExternalOutput")
    ds = dram("ds", [HALF, 8], F32, "ExternalOutput")
    pmax = dram("pmax", [128, 2], F32, "ExternalOutput")

    xn1T = dram("xn1T", [E, T], BF16)
    xn1To = dram("xn1To", [E, HALF], BF16)
    wq1B = dram("wq1B", [NH * DH, E], BF16)
    wk1B = dram("wk1B", [DH, E], BF16)
    wv1B = dram("wv1B", [DH, E], BF16)
    wo1B = dram("wo1B", [E, NH * DH], BF16)
    WfB = dram("WfB", [E, E], BF16)
    bvecB = dram("bvecB", [1, E], BF16)
    AmatB = dram("AmatB", [E, 32], BF16)
    W8B = dram("W8B", [8, E], BF16)
    limesB = dram("limesB", [8, 32], BF16)
    QT1 = dram("QT1", [NH * DH, T], BF16)
    KT1 = dram("KT1", [DH, T], BF16)
    V1 = dram("V1", [T, DH], BF16)
    CT1 = dram("CT1", [NH * DH, T], BF16)

    b.block = b.es.enter_context(nc.Block())
    for s in ("ld", "pe", "act", "dve", "st", "cast", "pool"):
        b.sem(s)

    with b.es:
        b.load_consts(["ident128", "ident8", "ones_col", "ones_row_f32",
                       "ones_row_bf", "alpha_row",
                       "rp1cq", "rp1sq", "rp1ck", "rp1sk"])
        _cast(b, wq1B, wq1)
        _cast(b, wk1B, wk1)
        _cast(b, wv1B, wv1)
        _cast(b, wo1B, wo1)
        _cast(b, WfB, Wf)
        _cast(b, bvecB, bvec)
        _cast(b, AmatB, Amat)
        _cast(b, W8B, W8)
        _cast(b, limesB, limes)

        emit_ln_transpose(b, [x_full, S0, S1i, Zfc], T, xn1T, "p0")
        emit_ln_transpose(b, [x_own, S0o, S1o, Zfo], HALF, xn1To, "p0o",
                          x1_out=x1_own)

        @b.block.sync
        def _(sync):
            b.wge(sync, "cast")
        b.barrier()

        emit_qkv(b, 1, xn1T, [
            {"rows": NH * DH, "wB": wq1B, "out": QT1, "fmajor": True,
             "rope": ("rp1cq", "rp1sq")},
            {"rows": DH, "wB": wk1B, "out": KT1, "fmajor": True,
             "rope": ("rp1ck", "rp1sk")},
            {"rows": DH, "wB": wv1B, "out": V1, "fmajor": False, "rope": None},
        ])

        emit_gate_pre(b, xn1T, AmatB, W8B, bvecB, limesB, zT, Hs, ds, pmax)

        emit_attn_l1(b, QT1, KT1, V1, CT1)

        emit_dense_tok(b, CT1, NH * DH, wo1B, S1_h, T, "wo1", slabT=2048)

        with ExitStack() as hes:
            bias_sb = b.sb("hbias", [1, E], BF16, hes)

            @b.block.sync
            def _(sync):
                b.inc(sync.dma_start(out=bias_sb[:], in_=bvecB), "ld", 16)
            b.barrier()
            emit_dense_tok(b, xn1To, E, WfB, Hout, HALF, "hff", slabT=1024,
                           bias_sb=bias_sb)

        @b.block.sync
        def _(sync):
            b.wge(sync, "st")
            b.wge(sync, "cast")
    return b.nc


def build_launch3(dbg=()):
    b = KB()
    nc = b.nc

    def dram(name, shape, dtype, kind="Internal"):
        kind2 = "ExternalOutput" if (name in dbg and kind == "Internal") else kind
        return nc.dram_tensor(name, shape, dtype, kind=kind2).ap()

    x1o = dram("x1o", [HALF, E], F32, "ExternalInput")
    S1a = dram("S1a", [HALF, E], F32, "ExternalInput")
    S1b = dram("S1b", [HALF, E], F32, "ExternalInput")
    Hin = dram("Hin", [HALF, E], F32, "ExternalInput")
    zTo = dram("zTo", [32, HALF], BF16, "ExternalInput")
    Hso = dram("Hso", [1024, 8], F32, "ExternalInput")
    dso = dram("dso", [1024, 8], F32, "ExternalInput")
    pmall = dram("pmall", [2, 1024], F32, "ExternalInput")
    Bmat = dram("Bmat", [32, E], F32, "ExternalInput")
    limes = dram("limes", [8, 32], F32, "ExternalInput")

    out = dram("out", [HALF, E], F32, "ExternalOutput")

    BmatB = dram("BmatB", [32, E], BF16)
    limesB = dram("limesB", [8, 32], BF16)

    b.block = b.es.enter_context(nc.Block())
    for s in ("ld", "pe", "act", "dve", "st", "cast", "pool"):
        b.sem(s)

    with b.es:
        b.load_consts(["ident128", "ident8", "ones_col", "ones_row_f32",
                       "ones_row_bf"])
        _cast(b, BmatB, Bmat)
        _cast(b, limesB, limes)

        ident = b.consts["ident128"]
        ones_rf = b.consts["ones_row_f32"]

        pmx = b.sb("l3pm", [2, 1024], F32)
        scl = b.sb("l3scl", [2, 4], F32)
        rH = b.sb("l3rH", [1, 1], F32)
        rD = b.sb("l3rD", [1, 1], F32)
        rHc = b.sb("l3rHc", [128, 1], F32)
        rDc = b.sb("l3rDc", [128, 1], F32)
        hs = b.sb("l3hs", [128, 8], F32)
        dss = b.sb("l3ds", [128, 8], F32)
        lg = b.sb("l3lg", [128, 8], F32)
        pr = b.sb("l3pr", [128, 8], F32)
        msk = b.sb("l3msk", [128, 8], F32)
        wf = b.sb("l3wf", [128, 8], BF16)
        red = b.sb("l3red", [128, 8], F32)
        wfT = b.sb("l3wfT", [8, 1024], BF16)
        wfull = b.sb("l3wfull", [8, HALF], BF16)
        limsb = b.sb("l3lim", [8, 32], BF16)
        bsb = b.sb("l3bm", [32, E], BF16)
        zsb = b.sb("l3z", [32, HALF], BF16)
        pmxT = b.sb("l3pT", [32, HALF], F32)
        zpm = b.sb("l3zpm", [32, HALF], BF16)
        xrow = b.sb("l3xr", [128, E], F32)
        srow = b.sb("l3sr", [128, E], F32)
        srow2 = b.sb("l3sr2", [128, E], F32)
        hrow = b.sb("l3hr", [128, E], F32)
        ostg = b.sb("l3os", [128, 512], F32)
        pA = b.ps("l3pA", [128, 512], F32)
        pB = b.ps("l3pB", [128, 512], F32)

        @b.block.sync
        def _(sync):
            b.wge(sync, "cast")
            b.inc(sync.dma_start(out=pmx[:], in_=pmall), "ld", 16)
            b.inc(sync.dma_start(out=limsb[:], in_=limesB), "ld", 16)
            b.inc(sync.dma_start(out=bsb[:], in_=BmatB), "ld", 16)
            b.inc(sync.dma_start(out=zsb[:], in_=zTo), "ld", 16)

        @b.block.vector
        def _(vector):
            b.wge(vector, "ld")
            b.inc(vector.reduce_max(scl[:, 0:1], pmx[:], axis=AX), "dve")
            b.wge(vector, "dve")
            b.inc(vector.tensor_scalar_max(scl[:, 1:2], scl[:, 0:1], 1e-6), "dve")
            b.wge(vector, "dve")
            b.inc(vector.reciprocal(scl[:, 2:3], scl[:, 1:2]), "dve")
            b.wge(vector, "dve")
            b.inc(vector.tensor_scalar_mul(scl[:, 3:4], scl[:, 2:3], 0.5), "dve")

        @b.block.gpsimd
        def _(gpsimd):
            b.wge(gpsimd, "dve")
            b.inc(gpsimd.dma_start(out=rH[:], in_=scl[0:1, 3:4]), "st", 16)
            b.inc(gpsimd.dma_start(out=rD[:], in_=scl[1:2, 3:4]), "st", 16)

        @b.block.tensor
        def _(tensor):
            b.wge(tensor, "st")
            b.inc(tensor.matmul(pA[:, 0:1], ones_rf[:], rH[:], start=True, stop=True), "pe")
            b.inc(tensor.matmul(pA[:, 1:2], ones_rf[:], rD[:], start=True, stop=True), "pe")

        @b.block.vector
        def _(vector):
            b.wge(vector, "pe")
            b.inc(vector.tensor_copy(rHc[:], pA[:, 0:1]), "dve")
            b.inc(vector.tensor_copy(rDc[:], pA[:, 1:2]), "dve")
        b.barrier()

        for at in range(1024 // 128):
            @b.block.sync
            def _(sync, at=at):
                b.wge(sync, "pe")
                b.wge(sync, "dve")
                b.inc(sync.dma_start(out=hs[:], in_=Hso[at * 128:(at + 1) * 128, :]), "ld", 16)
                b.inc(sync.dma_start(out=dss[:], in_=dso[at * 128:(at + 1) * 128, :]), "ld", 16)

            @b.block.vector
            def _(vector):
                b.wge(vector, "ld")
                b.inc(vector.tensor_scalar(out=lg[:], in0=hs[:], scalar1=rHc[:, 0:1],
                                           scalar2=None, op0=ALU.mult), "dve")
                b.inc(vector.tensor_scalar(out=red[:], in0=dss[:], scalar1=rDc[:, 0:1],
                                           scalar2=None, op0=ALU.mult), "dve")
                b.wge(vector, "dve")
                b.inc(vector.tensor_add(lg[:], lg[:], red[:]), "dve")
                b.wge(vector, "dve")
                b.inc(vector.reduce_max(red[:, 0:1], lg[:], axis=AX), "dve")

            @b.block.scalar
            def _(scalar):
                b.wge(scalar, "dve")
                b.inc(scalar.activation(out=red[:, 1:2], in_=red[:, 0:1], func=AF.Copy,
                                        scale=-1.0), "act")

            @b.block.scalar
            def _(scalar):
                b.wge(scalar, "act")
                b.inc(scalar.activation(out=pr[:], in_=lg[:], func=AF.Exp,
                                        bias=red[:, 1:2], accum_out=red[:, 2:3]), "act")

            @b.block.vector
            def _(vector):
                b.wge(vector, "act")
                b.inc(vector.reciprocal(red[:, 3:4], red[:, 2:3]), "dve")
                b.wge(vector, "dve")
                b.inc(vector.tensor_scalar(out=pr[:], in0=pr[:], scalar1=red[:, 3:4],
                                           scalar2=None, op0=ALU.mult), "dve")
                b.wge(vector, "dve")
                b.inc(vector.reduce_max(red[:, 4:5], pr[:], axis=AX), "dve")
                b.wge(vector, "dve")
                b.inc(vector.tensor_scalar(out=msk[:], in0=pr[:], scalar1=red[:, 4:5],
                                           scalar2=None, op0=ALU.is_ge), "dve")
                b.wge(vector, "dve")
                b.inc(vector.reduce_sum(red[:, 5:6], msk[:], axis=AX), "dve")
                b.inc(vector.scalar_tensor_tensor(out=red[:, 6:7] if False else lg[:],
                                                  in0=msk[:], scalar=-1e30, in1=pr[:],
                                                  op0=ALU.mult, op1=ALU.add), "dve")
                b.wge(vector, "dve")
                b.inc(vector.reduce_max(red[:, 6:7], lg[:], axis=AX), "dve")
                b.wge(vector, "dve")
                b.inc(vector.tensor_scalar(out=red[:, 7:8], in0=red[:, 5:6], scalar1=2.0,
                                           scalar2=None, op0=ALU.is_ge), "dve")
                b.wge(vector, "dve")
                b.inc(vector.tensor_sub(red[:, 5:6], red[:, 4:5], red[:, 6:7]), "dve")
                b.wge(vector, "dve")
                b.inc(vector.tensor_mul(red[:, 5:6], red[:, 5:6], red[:, 7:8]), "dve")
                b.wge(vector, "dve")
                b.inc(vector.tensor_add(red[:, 6:7], red[:, 6:7], red[:, 5:6]), "dve")
                b.wge(vector, "dve")
                b.inc(vector.tensor_scalar_mul(red[:, 7:8], red[:, 6:7], -2.0), "dve")

            @b.block.scalar
            def _(scalar):
                b.wge(scalar, "dve")
                b.inc(scalar.activation(out=msk[:], in_=pr[:], func=AF.Sigmoid,
                                        scale=2.0, bias=red[:, 7:8]), "act")

            @b.block.vector
            def _(vector):
                b.wge(vector, "act")
                b.inc(vector.tensor_mul(msk[:], msk[:], pr[:]), "dve")
                b.wge(vector, "dve")
                b.inc(vector.reduce_sum(red[:, 0:1], msk[:], axis=AX), "dve")
                b.wge(vector, "dve")
                b.inc(vector.tensor_scalar_add(red[:, 0:1], red[:, 0:1], 1e-9), "dve")
                b.wge(vector, "dve")
                b.inc(vector.reciprocal(red[:, 1:2], red[:, 0:1]), "dve")
                b.wge(vector, "dve")
                b.inc(vector.tensor_scalar(out=wf[:], in0=msk[:], scalar1=red[:, 1:2],
                                           scalar2=None, op0=ALU.mult), "dve")

            @b.block.tensor
            def _(tensor, at=at):
                b.wge(tensor, "dve")
                b.inc(tensor.matmul(pB[0:8, at % 4 * 128:(at % 4 + 1) * 128],
                                    wf[:], ident[:], start=True, stop=True), "pe")

            if at % 4 == 3:
                @b.block.vector
                def _(vector, at=at):
                    b.wge(vector, "pe")
                    b.inc(vector.tensor_copy(wfT[0:8, (at - 3) * 128:(at + 1) * 128],
                                             pB[0:8, 0:512]), "dve")
        b.barrier()

        @b.block.vector
        def _(vector):
            b.inc(vector.tensor_copy(wfull[:, 0:HALF:2], wfT[:]), "dve")
            b.inc(vector.tensor_copy(wfull[:, 1:HALF:2], wfT[:]), "dve")
        b.barrier()

        for tgo in range(HALF // 512):
            @b.block.tensor
            def _(tensor, tgo=tgo):
                b.wge(tensor, "dve")
                b.wge(tensor, "act")
                b.inc(tensor.matmul(pA[0:32, :], limsb[:],
                                    wfull[:, tgo * 512:(tgo + 1) * 512],
                                    start=True, stop=True), "pe")

            @b.block.vector
            def _(vector, tgo=tgo):
                b.wge(vector, "pe")
                b.inc(vector.tensor_copy(pmxT[0:32, tgo * 512:(tgo + 1) * 512],
                                         pA[0:32, :]), "dve")
        b.barrier()

        @b.block.vector
        def _(vector):
            b.inc(vector.scalar_tensor_tensor(out=zpm[0:32, :], in0=zsb[0:32, :],
                                              scalar=0.5, in1=pmxT[0:32, :],
                                              op0=ALU.mult, op1=ALU.mult), "dve")
        b.barrier()

        for tt in range(HALF // 128):
            @b.block.sync
            def _(sync, tt=tt):
                b.wge(sync, "dve")
                b.inc(sync.dma_start(out=xrow[:], in_=x1o[tt * 128:(tt + 1) * 128, :]), "ld", 16)
                b.inc(sync.dma_start(out=srow[:], in_=S1a[tt * 128:(tt + 1) * 128, :]), "ld", 16)
                b.inc(sync.dma_start(out=srow2[:], in_=S1b[tt * 128:(tt + 1) * 128, :]), "ld", 16)
                b.inc(sync.dma_start(out=hrow[:], in_=Hin[tt * 128:(tt + 1) * 128, :]), "ld", 16)

            @b.block.vector
            def _(vector):
                b.wge(vector, "ld")
                b.inc(vector.tensor_add(xrow[:], xrow[:], srow[:]), "dve")
                b.wge(vector, "dve")
                b.inc(vector.tensor_add(xrow[:], xrow[:], srow2[:]), "dve")
                b.wge(vector, "dve")
                b.inc(vector.tensor_add(xrow[:], xrow[:], hrow[:]), "dve")

            for eg in range(8):
                @b.block.tensor
                def _(tensor, tt=tt, eg=eg):
                    b.wge(tensor, "dve")
                    b.wge(tensor, "act")
                    b.inc(tensor.matmul(pB[:], zpm[0:32, tt * 128:(tt + 1) * 128],
                                        bsb[0:32, eg * 512:(eg + 1) * 512],
                                        start=True, stop=True), "pe")

                @b.block.vector
                def _(vector, eg=eg):
                    b.wge(vector, "pe")
                    b.inc(vector.tensor_add(ostg[:], pB[:], xrow[:, eg * 512:(eg + 1) * 512]), "dve")

                @b.block.scalar
                def _(scalar, tt=tt, eg=eg):
                    b.wge(scalar, "dve")
                    b.inc(scalar.dma_start(out=out[tt * 128:(tt + 1) * 128,
                                                   eg * 512:(eg + 1) * 512],
                                           in_=ostg[:]), "st", 16)
                @b.block.vector
                def _(vector):
                    b.wge(vector, "st")

        @b.block.sync
        def _(sync):
            b.wge(sync, "st")
    return b.nc


_TABLES = None


def _tables():
    global _TABLES
    if _TABLES is None:
        _TABLES = _host_tables()
    return _TABLES


def _with_consts(m, names):
    tabs = _tables()
    for nm in names:
        m[nm] = tabs[nm]
    return m


_L1_CONSTS = ("ident128", "ones_col", "ones_row_f32", "cmask",
              "rp0cq", "rp0sq", "rp0ck", "rp0sk")
_L2_CONSTS = ("ident128", "ident8", "ones_col", "ones_row_f32", "ones_row_bf",
              "alpha_row", "rp1cq", "rp1sq", "rp1ck", "rp1sk")
_L3_CONSTS = ("ident128", "ident8", "ones_col", "ones_row_f32", "ones_row_bf")


def _l1_inmaps(inputs):
    maps = []
    for c in range(NCORES):
        p, h = c // 2, c % 2
        x = np.ascontiguousarray(np.asarray(inputs["L0_attn_x"][p]))
        m = {
            "x_full": x,
            "x_own": np.ascontiguousarray(x[h * HALF:(h + 1) * HALF]),
            "wq": np.ascontiguousarray(np.asarray(inputs["L0_attn_Wq"])[h * NH * DH:(h + 1) * NH * DH]),
            "wk": np.ascontiguousarray(np.asarray(inputs["L0_attn_Wk"])[h * NKV0 * DH:(h + 1) * NKV0 * DH]),
            "wv": np.ascontiguousarray(np.asarray(inputs["L0_attn_Wv"])[h * NKV0 * DH:(h + 1) * NKV0 * DH]),
            "wo": np.ascontiguousarray(np.asarray(inputs["L0_attn_Wo"])[:, h * NH * DH:(h + 1) * NH * DH]),
            "fw": np.asarray(inputs["L0_ffn_weight"]),
            "fws": np.asarray(inputs["L0_ffn_wscale"]),
            "fwb": np.asarray(inputs["L0_ffn_wbias"]),
        }
        maps.append(_with_consts(m, _L1_CONSTS))
    return maps


def _l2_inmaps(inputs, r1):
    maps = []
    for c in range(NCORES):
        p, h = c // 2, c % 2
        ca, cb = 2 * p, 2 * p + 1
        x = np.ascontiguousarray(np.asarray(inputs["L0_attn_x"][p]))
        S0 = np.asarray(r1[ca]["S_h"])
        S1 = np.asarray(r1[cb]["S_h"])
        Zfc = np.concatenate([np.asarray(r1[ca]["Zf"]), np.asarray(r1[cb]["Zf"])], axis=0)
        own = slice(h * HALF, (h + 1) * HALF)
        m = {
            "x_full": x, "S0": S0, "S1i": S1, "Zfc": Zfc,
            "x_own": np.ascontiguousarray(x[own]),
            "S0o": np.ascontiguousarray(S0[own]),
            "S1o": np.ascontiguousarray(S1[own]),
            "Zfo": np.ascontiguousarray(Zfc[own]),
            "wq1": np.ascontiguousarray(np.asarray(inputs["L1_attn_Wq"])[h * NH * DH:(h + 1) * NH * DH]),
            "wk1": np.asarray(inputs["L1_attn_Wk"]),
            "wv1": np.asarray(inputs["L1_attn_Wv"]),
            "wo1": np.ascontiguousarray(np.asarray(inputs["L1_attn_Wo"])[:, h * NH * DH:(h + 1) * NH * DH]),
            "Wf": np.asarray(inputs["L1_ffn_W"]),
            "bvec": np.asarray(inputs["L1_ffn_b"]).reshape(1, E),
            "Amat": np.asarray(inputs["L1_ffn_A"]),
            "W8": np.ascontiguousarray(np.asarray(inputs["L1_ffn_W"])[0:8]),
            "limes": np.asarray(inputs["L1_ffn_LiMEs"]),
        }
        maps.append(_with_consts(m, _L2_CONSTS))
    return maps


def _l3_inmaps(inputs, r2):
    pm = np.stack([np.asarray(r2[c]["pmax"]) for c in range(NCORES)])
    pmall = np.ascontiguousarray(pm.transpose(2, 0, 1).reshape(2, NCORES * 128)[:, 0:1024])
    maps = []
    for c in range(NCORES):
        p, h = c // 2, c % 2
        ca, cb = 2 * p, 2 * p + 1
        own = slice(h * HALF, (h + 1) * HALF)
        owna = slice(h * 1024, (h + 1) * 1024)
        m = {
            "x1o": np.asarray(r2[c]["x1_own"]),
            "S1a": np.ascontiguousarray(np.asarray(r2[ca]["S1_h"])[own]),
            "S1b": np.ascontiguousarray(np.asarray(r2[cb]["S1_h"])[own]),
            "Hin": np.asarray(r2[c]["Hout"]),
            "zTo": np.ascontiguousarray(np.asarray(r2[c]["zT"])[:, own]),
            "Hso": np.ascontiguousarray(np.asarray(r2[c]["Hs"])[owna]),
            "dso": np.ascontiguousarray(np.asarray(r2[c]["ds"])[owna]),
            "pmall": pmall,
            "Bmat": np.asarray(inputs["L1_ffn_Bmat"]),
            "limes": np.asarray(inputs["L1_ffn_LiMEs"]),
        }
        maps.append(_with_consts(m, _L3_CONSTS))
    return maps


_NC_CACHE = {}


def _get_nc(which):
    if which not in _NC_CACHE:
        _NC_CACHE[which] = {"l1": build_launch1, "l2": build_launch2,
                            "l3": build_launch3}[which]()
    return _NC_CACHE[which]


def kernel(**inputs):
    cores = list(range(NCORES))
    r1 = run_bass_kernel_spmd(_get_nc("l1"), _l1_inmaps(inputs), core_ids=cores).results
    r2 = run_bass_kernel_spmd(_get_nc("l2"), _l2_inmaps(inputs, r1), core_ids=cores).results
    r3 = run_bass_kernel_spmd(_get_nc("l3"), _l3_inmaps(inputs, r2), core_ids=cores).results
    out = np.empty((4, T, E), np.float32)
    for c in range(NCORES):
        p, h = c // 2, c % 2
        out[p, h * HALF:(h + 1) * HALF] = np.asarray(r3[c]["out"])
    return out


# version 05 (final)
# speedup vs baseline: 1.6436x; 1.6436x over previous
import math
from contextlib import ExitStack

import numpy as np
import ml_dtypes

import concourse.bass as bass
import concourse.mybir as mybir
from concourse.bass_utils import run_bass_kernel_spmd

F32 = mybir.dt.float32
BF16 = mybir.dt.bfloat16
AF = mybir.ActivationFunctionType
ALU = mybir.AluOpType
AX = mybir.AxisListType.X
BF = ml_dtypes.bfloat16

T = 4096
E = 4096
DH = 128
NH = 16
NKV0 = 4
HALF = 2048
HWIN = 256
NCORES = 8


def _rope0_tables():
    ROPE_THETA = 500000.0
    ROPE_FACTOR = 32.0
    HI, LO = 4.0, 1.0
    ORIG = 8192
    inv = 1.0 / ROPE_THETA ** (np.arange(0, DH, 2, dtype=np.float64) / DH)
    wl = 2.0 * math.pi / inv
    inv2 = np.where(wl > ORIG / LO, inv / ROPE_FACTOR, inv)
    smooth = (ORIG / wl - LO) / (HI - LO)
    smoothed = (1.0 - smooth) * inv2 / ROPE_FACTOR + smooth * inv2
    is_med = (~(wl < ORIG / LO)) & (~(wl > ORIG / HI))
    inv3 = np.where(is_med, smoothed, inv2)
    pos = np.arange(T, dtype=np.float64)
    f = pos[:, None] * inv3[None, :]
    emb = np.concatenate([f, f], axis=1)
    return np.cos(emb).T.astype(np.float32), np.sin(emb).T.astype(np.float32)


def _rope1_tables():
    base = 10000.0
    inv = 1.0 / base ** (np.arange(0, DH, 2, dtype=np.float64) / DH)
    t = np.arange(T, dtype=np.float64)
    f = np.outer(t, inv)
    cos = np.zeros((T, DH))
    sin = np.zeros((T, DH))
    cos[:, 0::2] = np.cos(f)
    cos[:, 1::2] = np.cos(f)
    sin[:, 0::2] = np.sin(f)
    sin[:, 1::2] = np.sin(f)
    return cos.T.astype(np.float32), sin.T.astype(np.float32)


def _host_tables():
    out = {}
    s = 1.0 / math.sqrt(DH)
    for lid, (cos, sin) in enumerate([_rope0_tables(), _rope1_tables()]):
        sgn = np.ones((DH, 1), np.float32)
        sgn[:64] = -1.0
        sinS = sin * sgn
        out[f"rp{lid}cq"] = (cos * s).astype(BF)
        out[f"rp{lid}sq"] = (sinS * s).astype(BF)
        out[f"rp{lid}ck"] = cos.astype(BF)
        out[f"rp{lid}sk"] = sinS.astype(BF)
    out["ident128"] = np.eye(128, dtype=BF)
    out["ident8"] = np.eye(8, dtype=BF)
    out["ones_col"] = np.ones((128, 1), dtype=BF)
    out["ones_row_f32"] = np.ones((1, 128), np.float32)
    out["ones_row_bf"] = np.ones((1, 128), dtype=BF)
    cm = np.zeros((128, 4 * 512), np.float32)
    r = np.arange(128)[:, None]
    c = np.arange(512)[None, :]
    for qt in range(4):
        cm[:, qt * 512:(qt + 1) * 512] = np.where(c <= qt * 128 + r, 0.0, -1e30)
    out["cmask"] = cm
    al = np.linspace(0.0, 1.0, HWIN, dtype=np.float32)[None, :]
    out["alpha_row"] = np.broadcast_to(al, (128, HWIN)).copy()
    return out


_CONST_SPECS = {
    "rp0cq": ([DH, T], BF16), "rp0sq": ([DH, T], BF16),
    "rp0ck": ([DH, T], BF16), "rp0sk": ([DH, T], BF16),
    "rp1cq": ([DH, T], BF16), "rp1sq": ([DH, T], BF16),
    "rp1ck": ([DH, T], BF16), "rp1sk": ([DH, T], BF16),
    "ident128": ([128, 128], BF16),
    "ident8": ([8, 8], BF16),
    "ones_col": ([128, 1], BF16),
    "ones_row_f32": ([1, 128], F32),
    "ones_row_bf": ([1, 128], BF16),
    "cmask": ([128, 2048], F32),
    "alpha_row": ([128, HWIN], F32),
}


class KB:
    def __init__(self):
        self.nc = bass.Bass("TRN2", target_bir_lowering=False, debug=False)
        self.es = ExitStack()
        self.sems = {}
        self.block = None
        self.consts = {}

    def sem(self, name):
        h = self.es.enter_context(self.nc.semaphore(name))
        self.sems[name] = [h, 0]
        return h

    def inc(self, instr, name, k=1):
        s = self.sems[name]
        instr.then_inc(s[0], k)
        s[1] += k
        return instr

    def n(self, name):
        return self.sems[name][1]

    def wge(self, eng, name, val=None):
        s = self.sems[name]
        v = s[1] if val is None else val
        if v > 0:
            eng.wait_ge(s[0], v)

    def barrier(self):
        targets = [(s[0], s[1]) for s in self.sems.values() if s[1] > 0]
        for engf in (self.block.sync, self.block.tensor, self.block.scalar,
                     self.block.vector, self.block.gpsimd):
            @engf
            def _(eng, targets=targets):
                for h, v in targets:
                    eng.wait_ge(h, v)

    def sb(self, name, shape, dtype, es=None):
        return (es or self.es).enter_context(self.nc.sbuf_tensor(name, shape, dtype))

    def ps(self, name, shape, dtype=F32, es=None):
        return (es or self.es).enter_context(self.nc.psum_tensor(name, shape, dtype))

    def load_consts(self, names):
        aps = {}
        for name in names:
            shape, dt = _CONST_SPECS[name]
            aps[name] = self.nc.dram_tensor(name, shape, dt, kind="ExternalInput").ap()
            self.consts[name] = self.sb("c_" + name, shape, dt)

        @self.block.sync
        def _(sync):
            for name in aps:
                self.inc(sync.dma_start(out=self.consts[name][:], in_=aps[name]), "ld", 16)


def _cast(b, dst, src):
    @b.block.gpsimd
    def _(gpsimd):
        b.inc(gpsimd.dma_start(out=dst, in_=src), "cast", 16)


def emit_ln_transpose(b, src_aps, R, xnT, tag, x1_out=None):
    nsrc = len(src_aps)
    with ExitStack() as pes:
        A = [b.sb(f"{tag}A{j}", [128, E], F32, pes) for j in range(2)]
        Bc = [b.sb(f"{tag}B{j}", [128, E], F32, pes) for j in range(2)]
        SQ = b.sb(f"{tag}SQ", [128, E], F32, pes)
        XN = [b.sb(f"{tag}xn{j}", [128, E], BF16, pes) for j in range(8)]
        ST = [b.sb(f"{tag}st{j}", [128, 512], BF16, pes) for j in range(2)]
        red = b.sb(f"{tag}red", [128, 16], F32, pes)
        ident = b.consts["ident128"]
        pT = [b.ps(f"{tag}pT{j}", [128, 512], F32, es=pes) for j in range(2)]

        ntiles = R // 128
        A_guard = [0, 0]
        A_guard_act = [0, 0]
        Bc_guard = [0, 0]
        XN_guard = [0] * 8
        xn_dve = [0] * 8
        acc_cnt = [0, 0]
        pT_guard = [0, 0]
        ST_guard = [0, 0]

        def tile_chain(tt):
            j = tt % 2
            j8 = tt % 8
            r0 = tt * 128

            @b.block.sync
            def _(sync, tt=tt, j=j):
                if A_guard[j]:
                    b.wge(sync, "dve", A_guard[j])
                if A_guard_act[j]:
                    b.wge(sync, "act", A_guard_act[j])
                b.wge(sync, "st")
                b.inc(sync.dma_start(out=A[j][:], in_=src_aps[0][r0:r0 + 128, :]), "ld", 16)
            base_ld = b.n("ld")

            if nsrc > 1:
                @b.block.gpsimd
                def _(gpsimd, tt=tt, j=j, base_ld=base_ld):
                    b.wge(gpsimd, "ld", base_ld)
                    for k in range(1, nsrc):
                        b.inc(gpsimd.dma_start(out=A[j][:, 0:2048],
                                               in_=src_aps[k][r0:r0 + 128, 0:2048],
                                               accum_op=ALU.add), "acc", 16)
                        b.inc(gpsimd.dma_start(out=A[j][:, 2048:4096],
                                               in_=src_aps[k][r0:r0 + 128, 2048:4096],
                                               accum_op=ALU.add), "acc", 16)
                        if k < nsrc - 1:
                            b.wge(gpsimd, "acc")
            acc_cnt[j] = b.n("acc")

            if x1_out is not None:
                @b.block.gpsimd
                def _(gpsimd, tt=tt, j=j):
                    b.wge(gpsimd, "acc", acc_cnt[j])
                    b.inc(gpsimd.dma_start(out=x1_out[r0:r0 + 128, :], in_=A[j][:]), "st", 16)

            @b.block.vector
            def _(vector, j=j, base_ld=base_ld):
                b.wge(vector, "ld", base_ld)
                if nsrc > 1:
                    b.wge(vector, "acc", acc_cnt[j])
                if Bc_guard[j]:
                    b.wge(vector, "act", Bc_guard[j])
                b.inc(vector.reduce_sum(red[:, j * 8:j * 8 + 1], A[j][:], axis=AX), "dve")
                b.wge(vector, "dve")
                b.inc(vector.tensor_scalar_mul(red[:, j * 8 + 1:j * 8 + 2],
                                               red[:, j * 8:j * 8 + 1], 1.0 / E), "dve")
                b.wge(vector, "dve")
                b.inc(vector.tensor_scalar(out=Bc[j][:], in0=A[j][:],
                                           scalar1=red[:, j * 8 + 1:j * 8 + 2],
                                           scalar2=None, op0=ALU.subtract), "dve")
            dve_c = b.n("dve")

            @b.block.scalar
            def _(scalar, j=j, dve_c=dve_c):
                b.wge(scalar, "dve", dve_c)
                b.inc(scalar.activation(out=SQ[:], in_=Bc[j][:], func=AF.Square,
                                        accum_out=red[:, j * 8 + 2:j * 8 + 3]), "act")
            A_guard_act[j] = b.n("act")
            Bc_guard[j] = b.n("act")

            @b.block.vector
            def _(vector, j=j):
                b.wge(vector, "act")
                b.inc(vector.tensor_scalar(out=red[:, j * 8 + 3:j * 8 + 4],
                                           in0=red[:, j * 8 + 2:j * 8 + 3],
                                           scalar1=1.0 / E, scalar2=1e-5,
                                           op0=ALU.mult, op1=ALU.add), "dve")
            dve_v = b.n("dve")

            @b.block.scalar
            def _(scalar, j=j, dve_v=dve_v):
                b.wge(scalar, "dve", dve_v)
                b.inc(scalar.activation(out=red[:, j * 8 + 4:j * 8 + 5],
                                        in_=red[:, j * 8 + 3:j * 8 + 4], func=AF.Sqrt), "act")
            act_s = b.n("act")

            @b.block.vector
            def _(vector, j=j, j8=j8, act_s=act_s):
                b.wge(vector, "act", act_s)
                b.inc(vector.reciprocal(red[:, j * 8 + 5:j * 8 + 6],
                                        red[:, j * 8 + 4:j * 8 + 5]), "dve")
                b.wge(vector, "dve")
                if XN_guard[j8]:
                    b.wge(vector, "pe", XN_guard[j8])
                b.inc(vector.tensor_scalar(out=XN[j8][:], in0=Bc[j][:],
                                           scalar1=red[:, j * 8 + 5:j * 8 + 6],
                                           scalar2=None, op0=ALU.mult), "dve")
            A_guard[j] = b.n("dve")
            xn_dve[j8] = b.n("dve")

        def transpose_st(s):
            slots = [(4 * s + i) % 8 for i in range(4)]
            need_dve = max(xn_dve[j8] for j8 in slots)
            for kc in range(32):
                pj = kc % 2
                stj = kc % 2

                @b.block.tensor
                def _(tensor, s=s, kc=kc, pj=pj, need_dve=need_dve):
                    b.wge(tensor, "dve", need_dve)
                    if pT_guard[pj]:
                        b.wge(tensor, "act", pT_guard[pj])
                    ins = None
                    for i in range(4):
                        ins = tensor.matmul(pT[pj][:, i * 128:(i + 1) * 128],
                                            XN[slots[i]][:, kc * 128:(kc + 1) * 128],
                                            ident[:], start=True, stop=True)
                    b.inc(ins, "pe")
                pe_t = b.n("pe")
                for j8 in slots:
                    XN_guard[j8] = pe_t

                @b.block.scalar
                def _(scalar, pj=pj, stj=stj, pe_t=pe_t):
                    b.wge(scalar, "pe", pe_t)
                    if ST_guard[stj]:
                        b.wge(scalar, "st", ST_guard[stj])
                    b.inc(scalar.activation(out=ST[stj][:], in_=pT[pj][:], func=AF.Copy), "act")
                pT_guard[pj] = b.n("act")
                act_e = b.n("act")

                @b.block.gpsimd
                def _(gpsimd, s=s, kc=kc, stj=stj, act_e=act_e):
                    b.wge(gpsimd, "act", act_e)
                    b.inc(gpsimd.dma_start(
                        out=xnT[kc * 128:(kc + 1) * 128, s * 512:(s + 1) * 512],
                        in_=ST[stj][:]), "st", 16)
                ST_guard[stj] = b.n("st")

        nst = R // 512
        for tt in range(min(4, ntiles)):
            tile_chain(tt)
        for s in range(nst):
            for tt in range(4 * s + 4, min(4 * s + 8, ntiles)):
                tile_chain(tt)
            transpose_st(s)
        b.barrier()


def emit_qkv(b, lid, xnT, wparts):
    with ExitStack() as pes:
        slab = [b.sb(f"qs{lid}_{j}", [128, 32 * 512], BF16, pes) for j in range(2)]
        wbuf = [b.sb(f"qw{lid}_{j}", [128, 32 * 512], BF16, pes) for j in range(2)]
        qh = [b.sb(f"qh{lid}_{j}", [128, 512], BF16, pes) for j in range(2)]
        rot = b.sb(f"qr{lid}", [128, 512], BF16, pes)
        t1 = b.sb(f"qt1{lid}", [128, 512], BF16, pes)
        pP = [b.ps(f"qp{lid}_{j}", [128, 512], F32, es=pes) for j in range(2)]

        parts = []
        for part in wparts:
            rows = part["rows"]
            GW = min(512, rows)
            for ng in range(rows // GW):
                parts.append((part, ng, GW))
        NU = len(parts)
        ntg = T // 512

        slab_ld = {}
        slab_guard = [0, 0]
        w_ld = {}
        w_guard = [0, 0]
        qh_guard = [0, 0]
        pP_guard = [0, 0]

        def load_slab(tg):
            j = tg % 2

            @b.block.sync
            def _(sync, tg=tg, j=j):
                if slab_guard[j]:
                    b.wge(sync, "pe", slab_guard[j])
                t0 = tg * 512
                for kc in range(32):
                    b.inc(sync.dma_start(out=slab[j][:, kc * 512:(kc + 1) * 512],
                                         in_=xnT[kc * 128:(kc + 1) * 128, t0:t0 + 512]), "ld", 16)
            slab_ld[tg] = b.n("ld")

        def load_w(tg, u):
            part, ng, GW = parts[u]
            j = (tg * NU + u) % 2

            @b.block.sync
            def _(sync, part=part, ng=ng, GW=GW, j=j):
                if w_guard[j]:
                    b.wge(sync, "pe", w_guard[j])
                for kc in range(32):
                    b.inc(sync.dma_start_transpose(
                        out=wbuf[j][:, kc * GW:kc * GW + GW],
                        in_=part["wB"][ng * GW:(ng + 1) * GW,
                                       kc * 128:(kc + 1) * 128]), "ld", 16)
            w_ld[(tg, u)] = b.n("ld")

        load_slab(0)
        load_w(0, 0)
        gunit = 0
        for tg in range(ntg):
            t0 = tg * 512
            sj = tg % 2
            for u in range(NU):
                part, ng, GW = parts[u]
                wj = (tg * NU + u) % 2
                if u + 1 < NU:
                    load_w(tg, u + 1)
                elif tg + 1 < ntg:
                    load_slab(tg + 1)
                    load_w(tg + 1, 0)
                nsub = GW // 128 if part["fmajor"] else 4
                for sub in range(nsub):
                    pj = gunit % 2

                    @b.block.tensor
                    def _(tensor, part=part, ng=ng, GW=GW, sub=sub, sj=sj,
                          wj=wj, pj=pj, tg=tg, u=u):
                        b.wge(tensor, "ld", max(slab_ld[tg], w_ld[(tg, u)]))
                        if pP_guard[pj]:
                            b.wge(tensor, "act", pP_guard[pj])
                        ins = None
                        if part["fmajor"]:
                            for kc in range(32):
                                ins = tensor.matmul(
                                    pP[pj][:],
                                    wbuf[wj][:, kc * GW + sub * 128:kc * GW + (sub + 1) * 128],
                                    slab[sj][:, kc * 512:(kc + 1) * 512],
                                    start=(kc == 0), stop=(kc == 31))
                        else:
                            for kc in range(32):
                                ins = tensor.matmul(
                                    pP[pj][:, 0:GW],
                                    slab[sj][:, kc * 512 + sub * 128:kc * 512 + (sub + 1) * 128],
                                    wbuf[wj][:, kc * GW:kc * GW + GW],
                                    start=(kc == 0), stop=(kc == 31))
                        b.inc(ins, "pe")
                    pe_mm = b.n("pe")
                    if sub == nsub - 1:
                        w_guard[wj] = pe_mm
                        if u == NU - 1:
                            slab_guard[sj] = pe_mm

                    qj = gunit % 2

                    @b.block.scalar
                    def _(scalar, part=part, GW=GW, pj=pj, qj=qj, pe_mm=pe_mm):
                        b.wge(scalar, "pe", pe_mm)
                        if qh_guard[qj]:
                            b.wge(scalar, "st", qh_guard[qj])
                        if part["fmajor"]:
                            b.inc(scalar.activation(out=qh[qj][:], in_=pP[pj][:],
                                                    func=AF.Copy), "act")
                        else:
                            b.inc(scalar.activation(out=qh[qj][:, 0:GW],
                                                    in_=pP[pj][:, 0:GW], func=AF.Copy), "act")
                    pP_guard[pj] = b.n("act")
                    act_ev = b.n("act")

                    if part["rope"] is not None:
                        cs = b.consts[part["rope"][0]]
                        sn = b.consts[part["rope"][1]]

                        @b.block.vector
                        def _(vector, t0=t0, qj=qj, act_ev=act_ev):
                            b.wge(vector, "act", act_ev)
                            b.inc(vector.tensor_copy(rot[0:64, :], qh[qj][64:128, :]), "dve")
                            b.inc(vector.tensor_copy(rot[64:128, :], qh[qj][0:64, :]), "dve")
                            b.inc(vector.tensor_mul(t1[:], qh[qj][:], cs[:, t0:t0 + 512]), "dve")
                            b.inc(vector.tensor_mul(rot[:], rot[:], sn[:, t0:t0 + 512]), "dve")
                            b.inc(vector.tensor_add(qh[qj][:], t1[:], rot[:]), "dve")
                        wsem, wval = "dve", b.n("dve")
                    else:
                        wsem, wval = "act", act_ev

                    @b.block.gpsimd
                    def _(gpsimd, part=part, ng=ng, GW=GW, sub=sub, t0=t0,
                          qj=qj, wsem=wsem, wval=wval):
                        b.wge(gpsimd, wsem, wval)
                        if part["fmajor"]:
                            n0 = ng * GW + sub * 128
                            dst = part["out"][n0:n0 + 128, t0:t0 + 512]
                            src_ap = qh[qj][:]
                        else:
                            dst = part["out"][t0 + sub * 128:t0 + (sub + 1) * 128,
                                              ng * GW:(ng + 1) * GW]
                            src_ap = qh[qj][:, 0:GW]
                        b.inc(gpsimd.dma_start(out=dst, in_=src_ap), "st", 16)
                    qh_guard[qj] = b.n("st")
                    gunit += 1
        b.barrier()


def emit_attn_l0(b, QT, KT, V, CT):
    ident = b.consts["ident128"]
    ones_col = b.consts["ones_col"]
    ones_row = b.consts["ones_row_f32"]
    cmask = b.consts["cmask"]
    with ExitStack() as pes:
        Kg = b.sb("aKg", [128, T], BF16, pes)
        Vg = b.sb("aVg", [128, 32 * 128], BF16, pes)
        Qh = b.sb("aQh", [128, T], BF16, pes)
        sq = b.sb("asq", [128, 512], BF16, pes)
        P = [[b.sb(f"aP{qt}_{par}", [128, T], BF16, pes) for par in range(2)]
             for qt in range(4)]
        PTs = [b.sb(f"aPT{j}", [128, 512], BF16, pes) for j in range(2)]
        sums = b.sb("asums", [128, 64], F32, pes)
        rcp = b.sb("arcp", [128, 8], F32, pes)
        diag = b.sb("adiag", [128, 2 * 4 * 128], BF16, pes)
        biasB = b.sb("abias", [128, 1], F32, pes)
        scal = b.sb("ascal", [1, 8], F32, pes)
        cstg = b.sb("acst", [128, 512], BF16, pes)
        pS = [b.ps(f"apS{j}", [128, 512], F32, es=pes) for j in range(4)]
        pPT = [b.ps(f"apPT{j}", [128, 512], F32, es=pes) for j in range(2)]
        pCtx = [b.ps(f"apCtx{j}", [128, 512], F32, es=pes) for j in range(2)]

        gS = [0, 0, 0, 0]
        gPT = [0, 0]
        gPTs = [0, 0]
        gCtx = [0, 0]

        def colmax2(src_sbuf, ncols, out_slot):
            sqbuf = P[3][1]

            @b.block.scalar
            def _(scalar):
                b.wge(scalar, "pe")
                b.wge(scalar, "dve")
                b.inc(scalar.activation(out=sqbuf[:, 0:ncols], in_=src_sbuf[:, 0:ncols],
                                        func=AF.Square), "act")
            act_sq = b.n("act")
            nch = ncols // 512
            red_dve = [0] * nch

            def mmc(c):
                @b.block.tensor
                def _(tensor, c=c):
                    if c == 0:
                        b.wge(tensor, "act", act_sq)
                    if c >= 2 and red_dve[c - 2]:
                        b.wge(tensor, "dve", red_dve[c - 2])
                    b.inc(tensor.matmul(pPT[c % 2][0:1, :], ones_col[:],
                                        sqbuf[:, c * 512:(c + 1) * 512],
                                        start=True, stop=True), "pe")
                return b.n("pe")

            def redc(c, mm_pe):
                @b.block.vector
                def _(vector, c=c, mm_pe=mm_pe):
                    b.wge(vector, "pe", mm_pe)
                    b.inc(vector.reduce_max(scal[0:1, 6:7], pPT[c % 2][0:1, :], axis=AX), "dve")
                    b.wge(vector, "dve")
                    if c == 0:
                        b.inc(vector.tensor_copy(scal[0:1, out_slot:out_slot + 1],
                                                 scal[0:1, 6:7]), "dve")
                    else:
                        b.inc(vector.tensor_tensor(scal[0:1, out_slot:out_slot + 1],
                                                   scal[0:1, out_slot:out_slot + 1],
                                                   scal[0:1, 6:7], op=ALU.max), "dve")
                red_dve[c] = b.n("dve")

            pes_ = [0] * nch
            pes_[0] = mmc(0)
            for c in range(1, nch):
                pes_[c] = mmc(c)
                redc(c - 1, pes_[c - 1])
            redc(nch - 1, pes_[nch - 1])

        def passA_units(h, qg, drain=False):
            def gen():
                yield from _passA_gen(h, qg)
            if drain:
                for _x in gen():
                    pass
                return None
            return gen()

        def _passA_gen(h, qg):
            par = qg % 2
            nkg = qg + 1
            dve_guard = b.n("dve")
            for kg in range(nkg):
                for qt in range(4):
                    s = (kg * 4 + qt) % 4

                    @b.block.tensor
                    def _(tensor, qg=qg, kg=kg, qt=qt, s=s):
                        if gS[s]:
                            b.wge(tensor, "act", gS[s])
                        b.inc(tensor.matmul(
                            pS[s][:],
                            Qh[:, qg * 512 + qt * 128:qg * 512 + (qt + 1) * 128],
                            Kg[:, kg * 512:(kg + 1) * 512],
                            start=True, stop=True), "qk")
                    pe_after_qk = b.n("qk")

                    if kg == qg:
                        @b.block.vector
                        def _(vector, qt=qt, s=s, pe_after_qk=pe_after_qk):
                            b.wge(vector, "qk", pe_after_qk)
                            b.inc(vector.tensor_add(pS[s][:], pS[s][:],
                                                    cmask[:, qt * 512:(qt + 1) * 512]), "dve")
                        dve_after_mask = b.n("dve")
                    else:
                        dve_after_mask = 0

                    @b.block.scalar
                    def _(scalar, kg=kg, qt=qt, par=par, s=s,
                          pe_after_qk=pe_after_qk, dve_after_mask=dve_after_mask):
                        b.wge(scalar, "qk", pe_after_qk)
                        w = max(dve_after_mask, dve_guard)
                        if w:
                            b.wge(scalar, "dve", w)
                        b.inc(scalar.activation(
                            out=P[qt][par][:, kg * 512:(kg + 1) * 512],
                            in_=pS[s][:], func=AF.Exp, bias=biasB[:, 0:1]), "act")
                    gS[s] = b.n("act")
                    act_after_exp = b.n("act")

                    @b.block.vector
                    def _(vector, kg=kg, qt=qt, par=par,
                          act_after_exp=act_after_exp):
                        b.wge(vector, "act", act_after_exp)
                        b.inc(vector.reduce_sum(
                            sums[:, qt * 16 + par * 8 + kg:qt * 16 + par * 8 + kg + 1],
                            P[qt][par][:, kg * 512:(kg + 1) * 512], axis=AX), "dve")
                    yield None

        def diagblock(qg):
            par = qg % 2
            nkg = qg + 1

            @b.block.vector
            def _(vector, par=par, nkg=nkg):
                b.wge(vector, "act")
                b.wge(vector, "pe")
                b.wge(vector, "dve")
                for qt in range(4):
                    b.inc(vector.reduce_sum(rcp[:, par * 4 + qt:par * 4 + qt + 1],
                                            sums[:, qt * 16 + par * 8:qt * 16 + par * 8 + nkg],
                                            axis=AX), "dve")
                b.wge(vector, "dve")
                for qt in range(4):
                    b.inc(vector.reciprocal(rcp[:, par * 4 + qt:par * 4 + qt + 1],
                                            rcp[:, par * 4 + qt:par * 4 + qt + 1]), "dve")
                b.wge(vector, "dve")
                for qt in range(4):
                    b.inc(vector.tensor_scalar(
                        out=diag[:, par * 512 + qt * 128:par * 512 + (qt + 1) * 128],
                        in0=ident[:], scalar1=rcp[:, par * 4 + qt:par * 4 + qt + 1],
                        scalar2=None, op0=ALU.mult), "dve")

        def passB_units(h, qg, dve_after_diag):
            par = qg % 2
            nkg = qg + 1
            nunits = 4 * nkg
            evac_dve = [0] * nunits

            def emit_tmm(u):
                kg, tkc = u // 4, u % 4
                j = u % 2

                @b.block.tensor
                def _(tensor, kg=kg, tkc=tkc, j=j):
                    w = max(dve_after_diag, gPT[j])
                    if w:
                        b.wge(tensor, "dve", w)
                    ins = None
                    for qt in range(4):
                        ins = tensor.matmul(
                            pPT[j][:, qt * 128:(qt + 1) * 128],
                            P[qt][par][:, kg * 512 + tkc * 128:kg * 512 + (tkc + 1) * 128],
                            diag[:, par * 512 + qt * 128:par * 512 + (qt + 1) * 128],
                            start=True, stop=True)
                    b.inc(ins, "pe")
                pe_after_t = b.n("pe")

                @b.block.vector
                def _(vector, j=j, pe_after_t=pe_after_t):
                    b.wge(vector, "pe", max(pe_after_t, gPTs[j]))
                    b.inc(vector.tensor_copy(PTs[j][:], pPT[j][:]), "dve")
                gPT[j] = b.n("dve")
                evac_dve[u] = b.n("dve")

            def emit_pv(u):
                kg, tkc = u // 4, u % 4
                j = u % 2

                @b.block.tensor
                def _(tensor, kg=kg, tkc=tkc, j=j, u=u):
                    b.wge(tensor, "dve", evac_dve[u])
                    if u == 0 and gCtx[par]:
                        b.wge(tensor, "act", gCtx[par])
                    b.inc(tensor.matmul(
                        pCtx[par][:], Vg[:, (kg * 4 + tkc) * 128:(kg * 4 + tkc + 1) * 128],
                        PTs[j][:], start=(u == 0), stop=(u == nunits - 1),
                        skip_group_check=True), "pe")
                gPTs[j] = b.n("pe")

            emit_tmm(0)
            yield None
            for u in range(1, nunits):
                emit_tmm(u)
                emit_pv(u - 1)
                yield None
            emit_pv(nunits - 1)
            pe_pv_last = b.n("pe")

            @b.block.scalar
            def _(scalar, par=par, pe_pv_last=pe_pv_last):
                b.wge(scalar, "pe", pe_pv_last)
                b.wge(scalar, "st")
                b.inc(scalar.activation(out=cstg[:], in_=pCtx[par][:], func=AF.Copy), "act")
            gCtx[par] = b.n("act")

            @b.block.gpsimd
            def _(gpsimd, h=h, qg=qg):
                b.wge(gpsimd, "act")
                b.inc(gpsimd.dma_start(
                    out=CT[h * 128:(h + 1) * 128, qg * 512:(qg + 1) * 512],
                    in_=cstg[:]), "st", 16)

        for g in range(NKV0):
            @b.block.sync
            def _(sync, g=g):
                b.wge(sync, "pe")
                b.wge(sync, "act")
                b.inc(sync.dma_start(out=Kg[:], in_=KT[g * 128:(g + 1) * 128, :]), "ld", 16)
                for kgc in range(32):
                    b.inc(sync.dma_start(out=Vg[:, kgc * 128:(kgc + 1) * 128],
                                         in_=V[kgc * 128:(kgc + 1) * 128,
                                               g * 128:(g + 1) * 128]), "ld", 16)
            b.barrier()
            colmax2(Kg, T, 1)
            b.barrier()

            for u in range(4):
                h = g * 4 + u

                @b.block.sync
                def _(sync, h=h):
                    b.wge(sync, "pe")
                    b.wge(sync, "act")
                    b.inc(sync.dma_start(out=Qh[:], in_=QT[h * 128:(h + 1) * 128, :]), "ld", 16)
                b.barrier()
                colmax2(Qh, T, 0)

                @b.block.vector
                def _(vector):
                    b.wge(vector, "dve")
                    b.inc(vector.tensor_mul(scal[0:1, 2:3], scal[0:1, 0:1], scal[0:1, 1:2]), "dve")

                @b.block.scalar
                def _(scalar):
                    b.wge(scalar, "dve")
                    b.inc(scalar.activation(out=scal[0:1, 3:4], in_=scal[0:1, 2:3],
                                            func=AF.Sqrt), "act")

                @b.block.vector
                def _(vector):
                    b.wge(vector, "act")
                    b.inc(vector.tensor_scalar_mul(scal[0:1, 4:5], scal[0:1, 3:4], -1.0), "dve")

                @b.block.tensor
                def _(tensor):
                    b.wge(tensor, "dve")
                    b.inc(tensor.matmul(pPT[0][:, 0:1], ones_row[:], scal[0:1, 4:5],
                                        start=True, stop=True), "pe")

                @b.block.vector
                def _(vector):
                    b.wge(vector, "pe")
                    b.inc(vector.tensor_copy(biasB[:], pPT[0][:, 0:1]), "dve")
                b.barrier()
                gS[0] = gS[1] = gS[2] = gS[3] = 0
                gPT[0] = gPT[1] = 0
                gPTs[0] = gPTs[1] = 0
                gCtx[0] = gCtx[1] = 0

                passA_units(h, 0, drain=True)
                for qg in range(8):
                    diagblock(qg)
                    dve_after_diag = b.n("dve")
                    genA = passA_units(h, qg + 1) if qg < 7 else None
                    genB = passB_units(h, qg, dve_after_diag)
                    alive = True
                    while alive:
                        if genA is not None:
                            try:
                                next(genA)
                            except StopIteration:
                                genA = None
                        try:
                            next(genB)
                        except StopIteration:
                            alive = False
                    if genA is not None:
                        for _x in genA:
                            pass
                b.barrier()


def emit_dense_tok(b, actT, K, wB, out, Rtok, tag, slabT=1024, bias_sb=None):
    nkc = K // 128
    with ExitStack() as pes:
        slab = b.sb(f"{tag}sl", [128, nkc * slabT], BF16, pes)
        wbuf = [b.sb(f"{tag}wb{j}", [128, nkc * 512], BF16, pes) for j in range(2)]
        stg = [b.sb(f"{tag}sg{j}", [128, 512], F32, pes) for j in range(2)]
        pD = [b.ps(f"{tag}pD{j}", [128, 512], F32, es=pes) for j in range(2)]

        nsl = Rtok // slabT
        w_ld = {}
        w_guard = [0, 0]
        slab_ld = {}
        stg_guard = [0, 0]
        pD_guard = [0, 0]

        def load_slab(sl):
            @b.block.sync
            def _(sync, sl=sl):
                b.wge(sync, "pe")
                tbase = sl * slabT
                for kc in range(nkc):
                    b.inc(sync.dma_start(out=slab[:, kc * slabT:(kc + 1) * slabT],
                                         in_=actT[kc * 128:(kc + 1) * 128,
                                                  tbase:tbase + slabT]), "ld", 16)
            slab_ld[sl] = b.n("ld")

        def load_w(sl, eg):
            j = (sl * 8 + eg) % 2

            @b.block.sync
            def _(sync, eg=eg, j=j):
                if w_guard[j]:
                    b.wge(sync, "pe", w_guard[j])
                for kc in range(nkc):
                    b.inc(sync.dma_start_transpose(
                        out=wbuf[j][:, kc * 512:(kc + 1) * 512],
                        in_=wB[eg * 512:(eg + 1) * 512, kc * 128:(kc + 1) * 128]), "ld", 16)
            w_ld[(sl, eg)] = b.n("ld")

        load_slab(0)
        load_w(0, 0)
        for sl in range(nsl):
            tbase = sl * slabT
            for eg in range(8):
                j = (sl * 8 + eg) % 2
                if eg + 1 < 8:
                    load_w(sl, eg + 1)
                for tt in range(slabT // 128):
                    pj = tt % 2

                    @b.block.tensor
                    def _(tensor, tt=tt, pj=pj, j=j, sl=sl, eg=eg):
                        b.wge(tensor, "ld", max(slab_ld[sl], w_ld[(sl, eg)]))
                        if pD_guard[pj]:
                            b.wge(tensor, "act", pD_guard[pj])
                        ins = None
                        for kc in range(nkc):
                            ins = tensor.matmul(
                                pD[pj][:],
                                slab[:, kc * slabT + tt * 128:kc * slabT + (tt + 1) * 128],
                                wbuf[j][:, kc * 512:(kc + 1) * 512],
                                start=(kc == 0),
                                stop=(kc == nkc - 1 and bias_sb is None))
                        if bias_sb is not None:
                            ins = tensor.matmul(
                                pD[pj][:], b.consts["ones_row_bf"][:],
                                bias_sb[0:1, eg * 512:(eg + 1) * 512],
                                start=False, stop=True, skip_group_check=True)
                        b.inc(ins, "pe")
                    pe_mm = b.n("pe")

                    @b.block.scalar
                    def _(scalar, pj=pj, tt=tt, pe_mm=pe_mm):
                        b.wge(scalar, "pe", pe_mm)
                        if stg_guard[tt % 2]:
                            b.wge(scalar, "st", stg_guard[tt % 2])
                        b.inc(scalar.activation(out=stg[tt % 2][:], in_=pD[pj][:],
                                                func=AF.Copy), "act")
                    pD_guard[pj] = b.n("act")
                    act_ev = b.n("act")
                    if tt == slabT // 128 - 1:
                        w_guard[j] = pe_mm

                    @b.block.gpsimd
                    def _(gpsimd, tbase=tbase, tt=tt, eg=eg, act_ev=act_ev):
                        b.wge(gpsimd, "act", act_ev)
                        b.inc(gpsimd.dma_start(
                            out=out[tbase + tt * 128:tbase + (tt + 1) * 128,
                                    eg * 512:(eg + 1) * 512],
                            in_=stg[tt % 2][:]), "st", 16)
                    stg_guard[tt % 2] = b.n("st")
            if sl + 1 < nsl:
                load_slab(sl + 1)
                load_w(sl + 1, 0)
        b.barrier()


def emit_ffn_w_build(b, fw, fws, fwb, fwB):
    with ExitStack() as pes:
        A = b.sb("fbA", [128, E], F32, pes)
        S = b.sb("fbS", [128, E], BF16, pes)
        W = b.sb("fbW", [128, E], BF16, pes)
        sc = b.sb("fbsc", [128, 2], F32, pes)

        for rb in range(32):
            @b.block.sync
            def _(sync, rb=rb):
                b.wge(sync, "act")
                b.wge(sync, "dve")
                b.wge(sync, "st")
                b.inc(sync.dma_start(out=A[:], in_=fw[rb * 128:(rb + 1) * 128, :]), "ld", 16)
                b.inc(sync.dma_start(out=sc[:, 0:1], in_=fws[rb * 128:(rb + 1) * 128, :]), "ld", 16)
                b.inc(sync.dma_start(out=sc[:, 1:2], in_=fwb[rb * 128:(rb + 1) * 128, :]), "ld", 16)

            @b.block.scalar
            def _(scalar):
                b.wge(scalar, "ld")
                b.wge(scalar, "dve")
                b.inc(scalar.activation(out=S[:], in_=A[:], func=AF.Sign), "act")

            @b.block.vector
            def _(vector):
                b.wge(vector, "act")
                b.wge(vector, "st")
                b.inc(vector.tensor_scalar(out=W[:], in0=S[:], scalar1=sc[:, 0:1],
                                           scalar2=sc[:, 1:2], op0=ALU.mult,
                                           op1=ALU.add), "dve")

            @b.block.gpsimd
            def _(gpsimd, rb=rb):
                b.wge(gpsimd, "dve")
                b.inc(gpsimd.dma_start(out=fwB[rb * 128:(rb + 1) * 128, :], in_=W[:]), "st", 16)
        b.barrier()


def build_launch1(dbg=()):
    b = KB()
    nc = b.nc

    def dram(name, shape, dtype, kind="Internal"):
        kind2 = "ExternalOutput" if (name in dbg and kind == "Internal") else kind
        return nc.dram_tensor(name, shape, dtype, kind=kind2).ap()

    x_full = dram("x_full", [T, E], F32, "ExternalInput")
    x_own = dram("x_own", [HALF, E], F32, "ExternalInput")
    wq = dram("wq", [NH * DH, E], F32, "ExternalInput")
    wk = dram("wk", [NKV0 * DH, E], F32, "ExternalInput")
    wv = dram("wv", [NKV0 * DH, E], F32, "ExternalInput")
    wo = dram("wo", [E, NH * DH], F32, "ExternalInput")
    fw = dram("fw", [E, E], F32, "ExternalInput")
    fws = dram("fws", [E, 1], F32, "ExternalInput")
    fwb = dram("fwb", [E, 1], F32, "ExternalInput")

    S_h = dram("S_h", [T, E], F32, "ExternalOutput")
    Zf = dram("Zf", [HALF, E], F32, "ExternalOutput")

    xnT = dram("xnT", [E, T], BF16)
    xnTo = dram("xnTo", [E, HALF], BF16)
    wqB = dram("wqB", [NH * DH, E], BF16)
    wkB = dram("wkB", [NKV0 * DH, E], BF16)
    wvB = dram("wvB", [NKV0 * DH, E], BF16)
    woB = dram("woB", [E, NH * DH], BF16)
    fwB = dram("fwB", [E, E], BF16)
    QT = dram("QT", [NH * DH, T], BF16)
    KT = dram("KT", [NKV0 * DH, T], BF16)
    V = dram("V", [T, NKV0 * DH], BF16)
    CT = dram("CT", [NH * DH, T], BF16)

    b.block = b.es.enter_context(nc.Block())
    for s in ("ld", "pe", "act", "dve", "st", "cast", "pool", "acc", "qk"):
        b.sem(s)

    with b.es:
        b.load_consts(["ident128", "ones_col", "ones_row_f32", "cmask",
                       "rp0cq", "rp0sq", "rp0ck", "rp0sk"])
        _cast(b, wqB, wq)
        _cast(b, wkB, wk)
        _cast(b, wvB, wv)
        _cast(b, woB, wo)

        with nc.named_scope("ln0"):
            emit_ln_transpose(b, [x_full], T, xnT, "p0")
        with nc.named_scope("ln0o"):
            emit_ln_transpose(b, [x_own], HALF, xnTo, "p0o")

        @b.block.sync
        def _(sync):
            b.wge(sync, "cast")
        b.barrier()

        with nc.named_scope("qkv0"):
            emit_qkv(b, 0, xnT, [
            {"rows": NH * DH, "wB": wqB, "out": QT, "fmajor": True,
             "rope": ("rp0cq", "rp0sq")},
            {"rows": NKV0 * DH, "wB": wkB, "out": KT, "fmajor": True,
             "rope": ("rp0ck", "rp0sk")},
            {"rows": NKV0 * DH, "wB": wvB, "out": V, "fmajor": False,
             "rope": None},
        ])

        with nc.named_scope("attn0"):
            emit_attn_l0(b, QT, KT, V, CT)

        with nc.named_scope("wo0"):
            emit_dense_tok(b, CT, NH * DH, woB, S_h, T, "wo", slabT=2048)

        with nc.named_scope("fwb"):
            emit_ffn_w_build(b, fw, fws, fwb, fwB)
        with nc.named_scope("ffn0"):
            emit_dense_tok(b, xnTo, E, fwB, Zf, HALF, "ffn", slabT=1024)

        @b.block.sync
        def _(sync):
            b.wge(sync, "st")
            b.wge(sync, "cast")
    return b.nc


def emit_gate_pre(b, xn1T, A_bf, W8bf, bvecB, limesB, zT, Hs, ds, pmax):
    ident8 = b.consts["ident8"]
    ones_rb = b.consts["ones_row_bf"]
    with ExitStack() as pes:
        slab = b.sb("gs", [128, 32 * 512], BF16, pes)
        asb = b.sb("gasb", [128, 32 * 32], BF16, pes)
        w8 = b.sb("gw8", [8, E], BF16, pes)
        w8T = b.sb("gw8T", [128, 32 * 8], BF16, pes)
        lim = b.sb("glim", [8, 32], BF16, pes)
        limT = b.sb("glimT", [32, 8], BF16, pes)
        b8 = b.sb("gb8", [1, 8], BF16, pes)
        zsb = b.sb("gz", [32, 512], BF16, pes)
        aslab = b.sb("gaslab", [128, 32 * 256], BF16, pes)
        za = b.sb("gza", [32, 256], BF16, pes)
        hs_sb = b.sb("ghs", [128, 8], F32, pes)
        ds_sb = b.sb("gds", [128, 8], F32, pes)
        pm = b.sb("gpm", [128, 4], F32, pes)
        stg = b.sb("gstg", [128, 8], BF16, pes)
        pZ = b.ps("gpZ", [128, 512], F32, es=pes)
        pH = b.ps("gpH", [128, 512], F32, es=pes)

        @b.block.sync
        def _(sync):
            for kc in range(32):
                b.inc(sync.dma_start(out=asb[:, kc * 32:(kc + 1) * 32],
                                     in_=A_bf[kc * 128:(kc + 1) * 128, :]), "ld", 16)
            b.inc(sync.dma_start(out=w8[:], in_=W8bf), "ld", 16)
            b.inc(sync.dma_start(out=lim[:], in_=limesB), "ld", 16)
            b.inc(sync.dma_start(out=b8[:], in_=bvecB[0:1, 0:8]), "ld", 16)

        @b.block.tensor
        def _(tensor):
            b.wge(tensor, "ld")
            for kc in range(32):
                b.inc(tensor.matmul(pH[:, kc * 8:(kc + 1) * 8] if kc < 64 else None,
                                    w8[:, kc * 128:(kc + 1) * 128], ident8[:],
                                    start=True, stop=True), "pe")
            b.inc(tensor.matmul(pZ[0:32, 0:8], lim[:], ident8[:],
                                start=True, stop=True), "pe")

        @b.block.vector
        def _(vector):
            b.wge(vector, "pe")
            b.inc(vector.tensor_copy(w8T[:], pH[:, 0:256]), "dve")
            b.inc(vector.tensor_copy(limT[:], pZ[0:32, 0:8]), "dve")
            b.inc(vector.memset(pm[:], 0.0), "dve")
        b.barrier()

        for tg in range(T // 512):
            t0 = tg * 512

            @b.block.sync
            def _(sync, t0=t0):
                b.wge(sync, "pe")
                for kc in range(32):
                    b.inc(sync.dma_start(out=slab[:, kc * 512:(kc + 1) * 512],
                                         in_=xn1T[kc * 128:(kc + 1) * 128, t0:t0 + 512]), "ld", 16)

            @b.block.vector
            def _(vector):
                b.wge(vector, "ld")
                b.wge(vector, "pe")
                ins = None
                for kc in range(32):
                    ins = vector.tensor_copy(aslab[:, kc * 256:(kc + 1) * 256],
                                             slab[:, kc * 512 + 1:kc * 512 + 512:2])
                b.inc(ins, "dve")

            @b.block.tensor
            def _(tensor):
                b.wge(tensor, "ld")
                b.wge(tensor, "act")
                ins = None
                for kc in range(32):
                    ins = tensor.matmul(pZ[0:32, :], asb[:, kc * 32:(kc + 1) * 32],
                                        slab[:, kc * 512:(kc + 1) * 512],
                                        start=(kc == 0), stop=(kc == 31))
                b.inc(ins, "pe")

            @b.block.scalar
            def _(scalar):
                b.wge(scalar, "pe")
                b.wge(scalar, "st")
                b.inc(scalar.activation(out=zsb[:], in_=pZ[0:32, :], func=AF.Copy), "act")

            @b.block.gpsimd
            def _(gpsimd, t0=t0):
                b.wge(gpsimd, "act")
                b.inc(gpsimd.dma_start(out=zT[0:32, t0:t0 + 512], in_=zsb[:]), "st", 16)

            @b.block.vector
            def _(vector):
                b.wge(vector, "act")
                b.wge(vector, "pe")
                b.inc(vector.tensor_copy(za[:], zsb[:, 1:512:2]), "dve")

            for at in range(2):
                a0 = t0 // 2 + at * 128
                c0 = at * 256 + 1

                @b.block.tensor
                def _(tensor, c0=c0):
                    b.wge(tensor, "act")
                    b.wge(tensor, "dve")
                    ins = None
                    for kc in range(32):
                        ins = tensor.matmul(
                            pH[:, 0:8],
                            aslab[:, kc * 256 + at * 128:kc * 256 + (at + 1) * 128],
                            w8T[:, kc * 8:(kc + 1) * 8],
                            start=(kc == 0), stop=False)
                    ins = tensor.matmul(pH[:, 0:8], ones_rb[:], b8[:],
                                        start=False, stop=True, skip_group_check=True)
                    b.inc(ins, "pe")
                    b.inc(tensor.matmul(pH[:, 8:16], za[:, at * 128:(at + 1) * 128],
                                        limT[:], start=True, stop=True), "pe")

                @b.block.vector
                def _(vector, a0=a0):
                    b.wge(vector, "pe")
                    b.wge(vector, "st")
                    b.inc(vector.tensor_copy(hs_sb[:], pH[:, 0:8]), "dve")
                    b.inc(vector.tensor_copy(ds_sb[:], pH[:, 8:16]), "dve")
                    b.wge(vector, "dve")
                    b.inc(vector.reduce_max(pm[:, 2:3], hs_sb[:], axis=AX,
                                            apply_absolute_value=True), "dve")
                    b.inc(vector.reduce_max(pm[:, 3:4], ds_sb[:], axis=AX,
                                            apply_absolute_value=True), "dve")
                    b.wge(vector, "dve")
                    b.inc(vector.tensor_tensor(pm[:, 0:1], pm[:, 0:1], pm[:, 2:3], op=ALU.max), "dve")
                    b.inc(vector.tensor_tensor(pm[:, 1:2], pm[:, 1:2], pm[:, 3:4], op=ALU.max), "dve")

                @b.block.gpsimd
                def _(gpsimd, a0=a0):
                    b.wge(gpsimd, "dve")
                    b.inc(gpsimd.dma_start(out=Hs[a0:a0 + 128, :], in_=hs_sb[:]), "st", 16)
                    b.inc(gpsimd.dma_start(out=ds[a0:a0 + 128, :], in_=ds_sb[:]), "st", 16)

        @b.block.gpsimd
        def _(gpsimd):
            b.wge(gpsimd, "dve")
            b.inc(gpsimd.dma_start(out=pmax, in_=pm[:, 0:2]), "st", 16)
        b.barrier()


def emit_attn_l1(b, QT1, KT1, V1, CT1):
    ident = b.consts["ident128"]
    alpha = b.consts["alpha_row"]
    with ExitStack() as pes:
        K1 = b.sb("wK1", [128, T], BF16, pes)
        Vg = b.sb("wVg", [128, 32 * 128], BF16, pes)
        Qh = b.sb("wQh", [128, T], BF16, pes)
        Pw = [[b.sb(f"wP{qt}_{par}", [128, 512], BF16, pes) for par in range(2)]
              for qt in range(4)]
        PTs = [b.sb(f"wPT{j}", [128, 512], BF16, pes) for j in range(2)]
        rcp = b.sb("wrcp", [128, 8], F32, pes)
        sums = b.sb("wsums", [128, 8], F32, pes)
        diag = b.sb("wdiag", [128, 2 * 4 * 128], BF16, pes)
        cw = [b.sb(f"wcw{j}", [128, 512], F32, pes) for j in range(2)]
        blnd = b.sb("wbl", [128, HWIN], F32, pes)
        t1 = b.sb("wt1", [128, HWIN], F32, pes)
        cstg = b.sb("wcst", [128, HWIN], BF16, pes)
        pS = [b.ps(f"wpS{j}", [128, 512], F32, es=pes) for j in range(4)]
        pPT = [b.ps(f"wpPT{j}", [128, 512], F32, es=pes) for j in range(2)]
        pCtx = [b.ps(f"wpCtx{j}", [128, 512], F32, es=pes) for j in range(2)]

        gS = [0, 0, 0, 0]
        gPT = [0, 0]
        gPTs = [0, 0]
        gCtx = [0, 0]
        gSums = [0, 0]

        @b.block.sync
        def _(sync):
            b.inc(sync.dma_start(out=K1[:], in_=KT1[0:128, :]), "ld", 16)
            for kgc in range(32):
                b.inc(sync.dma_start(out=Vg[:, kgc * 128:(kgc + 1) * 128],
                                     in_=V1[kgc * 128:(kgc + 1) * 128, 0:128]), "ld", 16)
        b.barrier()

        def wparams(wi):
            i = wi * HWIN
            end_ = min(i + 512, T)
            return i, end_, (end_ - i) // 128, end_ - i

        def passA(wi):
            i, end_, nq, nk = wparams(wi)
            par = wi % 2
            for qt in range(nq):
                s = (wi * 4 + qt) % 4

                @b.block.tensor
                def _(tensor, i=i, qt=qt, nk=nk, s=s):
                    if gS[s]:
                        b.wge(tensor, "act", gS[s])
                    b.inc(tensor.matmul(
                        pS[s][:, 0:nk],
                        Qh[:, i + qt * 128:i + (qt + 1) * 128],
                        K1[:, i:i + nk], start=True, stop=True), "qk")
                pe_after = b.n("qk")

                @b.block.scalar
                def _(scalar, qt=qt, nk=nk, par=par, s=s, pe_after=pe_after):
                    b.wge(scalar, "qk", pe_after)
                    if gSums[par]:
                        b.wge(scalar, "dve", gSums[par])
                    b.inc(scalar.activation(
                        out=Pw[qt][par][:, 0:nk], in_=pS[s][:, 0:nk],
                        func=AF.Sigmoid), "act")
                gS[s] = b.n("act")
                act_after_sig = b.n("act")

                @b.block.vector
                def _(vector, qt=qt, nk=nk, par=par,
                      act_after_sig=act_after_sig):
                    b.wge(vector, "act", act_after_sig)
                    b.inc(vector.reduce_sum(
                        sums[:, par * 4 + qt:par * 4 + qt + 1],
                        Pw[qt][par][:, 0:nk], axis=AX), "dve")

        def diagblock(wi):
            i, end_, nq, nk = wparams(wi)
            par = wi % 2

            @b.block.vector
            def _(vector, nq=nq, par=par):
                b.wge(vector, "act")
                b.wge(vector, "pe")
                b.wge(vector, "dve")
                for qt in range(nq):
                    b.inc(vector.reciprocal(rcp[:, par * 4 + qt:par * 4 + qt + 1],
                                            sums[:, par * 4 + qt:par * 4 + qt + 1]), "dve")
                b.wge(vector, "dve")
                for qt in range(nq):
                    b.inc(vector.tensor_scalar(
                        out=diag[:, par * 512 + qt * 128:par * 512 + (qt + 1) * 128],
                        in0=ident[:], scalar1=rcp[:, par * 4 + qt:par * 4 + qt + 1],
                        scalar2=None, op0=ALU.mult), "dve")
            gSums[par] = b.n("dve")

        def passB(h, wi, dve_after_diag):
            i, end_, nq, nk = wparams(wi)
            par = wi % 2
            nunits = nk // 128
            evac_dve = [0] * nunits

            def emit_tmm(u):
                j = u % 2

                @b.block.tensor
                def _(tensor, u=u, j=j):
                    w = max(dve_after_diag, gPT[j])
                    if w:
                        b.wge(tensor, "dve", w)
                    ins = None
                    for qt in range(nq):
                        ins = tensor.matmul(
                            pPT[j][:, qt * 128:(qt + 1) * 128],
                            Pw[qt][par][:, u * 128:(u + 1) * 128],
                            diag[:, par * 512 + qt * 128:par * 512 + (qt + 1) * 128],
                            start=True, stop=True)
                    b.inc(ins, "pe")
                pe_after_t = b.n("pe")

                @b.block.vector
                def _(vector, j=j, pe_after_t=pe_after_t):
                    b.wge(vector, "pe", max(pe_after_t, gPTs[j]))
                    b.inc(vector.tensor_copy(PTs[j][:, 0:nq * 128], pPT[j][:, 0:nq * 128]), "dve")
                gPT[j] = b.n("dve")
                evac_dve[u] = b.n("dve")

            def emit_pv(u):
                j = u % 2

                @b.block.tensor
                def _(tensor, u=u, j=j):
                    b.wge(tensor, "dve", evac_dve[u])
                    if u == 0 and gCtx[par]:
                        b.wge(tensor, "act", gCtx[par])
                    b.inc(tensor.matmul(
                        pCtx[par][:, 0:nq * 128],
                        Vg[:, (i // 128 + u) * 128:(i // 128 + u + 1) * 128],
                        PTs[j][:, 0:nq * 128], start=(u == 0), stop=(u == nunits - 1),
                        skip_group_check=True), "pe")
                gPTs[j] = b.n("pe")

            emit_tmm(0)
            for u in range(1, nunits):
                emit_tmm(u)
                emit_pv(u - 1)
            emit_pv(nunits - 1)

            @b.block.vector
            def _(vector, par=par, nq=nq, wi=wi):
                b.wge(vector, "pe")
                b.wge(vector, "act")
                b.inc(vector.tensor_copy(cw[par][:, 0:nq * 128], pCtx[par][:, 0:nq * 128]), "dve")
                b.wge(vector, "dve")
                if wi == 0:
                    b.inc(vector.tensor_copy(blnd[:], cw[par][:, 0:HWIN]), "dve")
                else:
                    b.inc(vector.tensor_sub(t1[:], cw[par][:, 0:HWIN],
                                            cw[1 - par][:, HWIN:512]), "dve")
                    b.wge(vector, "dve")
                    b.inc(vector.tensor_mul(t1[:], t1[:], alpha[:]), "dve")
                    b.wge(vector, "dve")
                    b.inc(vector.tensor_add(blnd[:], t1[:], cw[1 - par][:, HWIN:512]), "dve")
            gCtx[par] = 0

            dve_after_blend = b.n("dve")

            @b.block.scalar
            def _(scalar, dve_after_blend=dve_after_blend):
                b.wge(scalar, "dve", dve_after_blend)
                b.wge(scalar, "st")
                b.inc(scalar.activation(out=cstg[:], in_=blnd[:], func=AF.Copy), "act")
            gCtx[par] = b.n("act")

            @b.block.gpsimd
            def _(gpsimd, h=h, i=i):
                b.wge(gpsimd, "act")
                b.inc(gpsimd.dma_start(
                    out=CT1[h * 128:(h + 1) * 128, i:i + HWIN], in_=cstg[:]), "st", 16)

        for h in range(NH):
            @b.block.sync
            def _(sync, h=h):
                b.wge(sync, "pe")
                b.wge(sync, "act")
                b.inc(sync.dma_start(out=Qh[:], in_=QT1[h * 128:(h + 1) * 128, :]), "ld", 16)
            b.barrier()
            for s_ in range(4):
                gS[s_] = 0
            gPT[0] = gPT[1] = 0
            gPTs[0] = gPTs[1] = 0
            gCtx[0] = gCtx[1] = 0
            gSums[0] = gSums[1] = 0

            passA(0)
            for wi in range(16):
                diagblock(wi)
                dve_after_diag = b.n("dve")
                if wi < 15:
                    passA(wi + 1)
                passB(h, wi, dve_after_diag)
            b.barrier()


def build_launch2(dbg=()):
    b = KB()
    nc = b.nc

    def dram(name, shape, dtype, kind="Internal"):
        kind2 = "ExternalOutput" if (name in dbg and kind == "Internal") else kind
        return nc.dram_tensor(name, shape, dtype, kind=kind2).ap()

    x_full = dram("x_full", [T, E], F32, "ExternalInput")
    S0 = dram("S0", [T, E], F32, "ExternalInput")
    S1i = dram("S1i", [T, E], F32, "ExternalInput")
    Zfc = dram("Zfc", [T, E], F32, "ExternalInput")
    x_own = dram("x_own", [HALF, E], F32, "ExternalInput")
    S0o = dram("S0o", [HALF, E], F32, "ExternalInput")
    S1o = dram("S1o", [HALF, E], F32, "ExternalInput")
    Zfo = dram("Zfo", [HALF, E], F32, "ExternalInput")
    wq1 = dram("wq1", [NH * DH, E], F32, "ExternalInput")
    wk1 = dram("wk1", [DH, E], F32, "ExternalInput")
    wv1 = dram("wv1", [DH, E], F32, "ExternalInput")
    wo1 = dram("wo1", [E, NH * DH], F32, "ExternalInput")
    Wf = dram("Wf", [E, E], F32, "ExternalInput")
    bvec = dram("bvec", [1, E], F32, "ExternalInput")
    Amat = dram("Amat", [E, 32], F32, "ExternalInput")
    W8 = dram("W8", [8, E], F32, "ExternalInput")
    limes = dram("limes", [8, 32], F32, "ExternalInput")

    S1_h = dram("S1_h", [T, E], F32, "ExternalOutput")
    Hout = dram("Hout", [HALF, E], F32, "ExternalOutput")
    x1_own = dram("x1_own", [HALF, E], F32, "ExternalOutput")
    zT = dram("zT", [32, T], BF16, "ExternalOutput")
    Hs = dram("Hs", [HALF, 8], F32, "ExternalOutput")
    ds = dram("ds", [HALF, 8], F32, "ExternalOutput")
    pmax = dram("pmax", [128, 2], F32, "ExternalOutput")

    xn1T = dram("xn1T", [E, T], BF16)
    xn1To = dram("xn1To", [E, HALF], BF16)
    wq1B = dram("wq1B", [NH * DH, E], BF16)
    wk1B = dram("wk1B", [DH, E], BF16)
    wv1B = dram("wv1B", [DH, E], BF16)
    wo1B = dram("wo1B", [E, NH * DH], BF16)
    WfB = dram("WfB", [E, E], BF16)
    bvecB = dram("bvecB", [1, E], BF16)
    AmatB = dram("AmatB", [E, 32], BF16)
    W8B = dram("W8B", [8, E], BF16)
    limesB = dram("limesB", [8, 32], BF16)
    QT1 = dram("QT1", [NH * DH, T], BF16)
    KT1 = dram("KT1", [DH, T], BF16)
    V1 = dram("V1", [T, DH], BF16)
    CT1 = dram("CT1", [NH * DH, T], BF16)

    b.block = b.es.enter_context(nc.Block())
    for s in ("ld", "pe", "act", "dve", "st", "cast", "pool", "acc", "qk"):
        b.sem(s)

    with b.es:
        b.load_consts(["ident128", "ident8", "ones_col", "ones_row_f32",
                       "ones_row_bf", "alpha_row",
                       "rp1cq", "rp1sq", "rp1ck", "rp1sk"])
        _cast(b, wq1B, wq1)
        _cast(b, wk1B, wk1)
        _cast(b, wv1B, wv1)
        _cast(b, wo1B, wo1)
        _cast(b, WfB, Wf)
        _cast(b, bvecB, bvec)
        _cast(b, AmatB, Amat)
        _cast(b, W8B, W8)
        _cast(b, limesB, limes)

        with nc.named_scope("ln1"):
            emit_ln_transpose(b, [x_full, S0, S1i, Zfc], T, xn1T, "p0")
        with nc.named_scope("ln1o"):
            emit_ln_transpose(b, [x_own, S0o, S1o, Zfo], HALF, xn1To, "p0o",
                              x1_out=x1_own)

        @b.block.sync
        def _(sync):
            b.wge(sync, "cast")
        b.barrier()

        with nc.named_scope("qkv1"):
            emit_qkv(b, 1, xn1T, [
            {"rows": NH * DH, "wB": wq1B, "out": QT1, "fmajor": True,
             "rope": ("rp1cq", "rp1sq")},
            {"rows": DH, "wB": wk1B, "out": KT1, "fmajor": True,
             "rope": ("rp1ck", "rp1sk")},
            {"rows": DH, "wB": wv1B, "out": V1, "fmajor": False, "rope": None},
        ])

        with nc.named_scope("gate"):
            emit_gate_pre(b, xn1T, AmatB, W8B, bvecB, limesB, zT, Hs, ds, pmax)

        with nc.named_scope("attn1"):
            emit_attn_l1(b, QT1, KT1, V1, CT1)

        with nc.named_scope("wo1"):
            emit_dense_tok(b, CT1, NH * DH, wo1B, S1_h, T, "wo1", slabT=2048)

        with ExitStack() as hes:
            bias_sb = b.sb("hbias", [1, E], BF16, hes)

            @b.block.sync
            def _(sync):
                b.inc(sync.dma_start(out=bias_sb[:], in_=bvecB), "ld", 16)
            b.barrier()
            with nc.named_scope("hff"):
                emit_dense_tok(b, xn1To, E, WfB, Hout, HALF, "hff", slabT=1024,
                               bias_sb=bias_sb)

        @b.block.sync
        def _(sync):
            b.wge(sync, "st")
            b.wge(sync, "cast")
    return b.nc


def build_launch3(dbg=()):
    b = KB()
    nc = b.nc

    def dram(name, shape, dtype, kind="Internal"):
        kind2 = "ExternalOutput" if (name in dbg and kind == "Internal") else kind
        return nc.dram_tensor(name, shape, dtype, kind=kind2).ap()

    x1o = dram("x1o", [HALF, E], F32, "ExternalInput")
    S1a = dram("S1a", [HALF, E], F32, "ExternalInput")
    S1b = dram("S1b", [HALF, E], F32, "ExternalInput")
    Hin = dram("Hin", [HALF, E], F32, "ExternalInput")
    zTo = dram("zTo", [32, HALF], BF16, "ExternalInput")
    Hso = dram("Hso", [1024, 8], F32, "ExternalInput")
    dso = dram("dso", [1024, 8], F32, "ExternalInput")
    pmall = dram("pmall", [2, 1024], F32, "ExternalInput")
    Bmat = dram("Bmat", [32, E], F32, "ExternalInput")
    limes = dram("limes", [8, 32], F32, "ExternalInput")

    out = dram("out", [HALF, E], F32, "ExternalOutput")

    BmatB = dram("BmatB", [32, E], BF16)
    limesB = dram("limesB", [8, 32], BF16)

    b.block = b.es.enter_context(nc.Block())
    for s in ("ld", "pe", "act", "dve", "st", "cast", "pool", "acc", "qk"):
        b.sem(s)

    with b.es:
        b.load_consts(["ident128", "ident8", "ones_col", "ones_row_f32",
                       "ones_row_bf"])
        _cast(b, BmatB, Bmat)
        _cast(b, limesB, limes)

        ident = b.consts["ident128"]
        ones_rf = b.consts["ones_row_f32"]

        pmx = b.sb("l3pm", [2, 1024], F32)
        scl = b.sb("l3scl", [2, 4], F32)
        rH = b.sb("l3rH", [1, 1], F32)
        rD = b.sb("l3rD", [1, 1], F32)
        rHc = b.sb("l3rHc", [128, 1], F32)
        rDc = b.sb("l3rDc", [128, 1], F32)
        hs = b.sb("l3hs", [128, 8], F32)
        dss = b.sb("l3ds", [128, 8], F32)
        lg = b.sb("l3lg", [128, 8], F32)
        pr = b.sb("l3pr", [128, 8], F32)
        msk = b.sb("l3msk", [128, 8], F32)
        wf = b.sb("l3wf", [128, 8], BF16)
        red = b.sb("l3red", [128, 8], F32)
        wfT = b.sb("l3wfT", [8, 1024], BF16)
        wfull = b.sb("l3wfull", [8, HALF], BF16)
        limsb = b.sb("l3lim", [8, 32], BF16)
        bsb = b.sb("l3bm", [32, E], BF16)
        zsb = b.sb("l3z", [32, HALF], BF16)
        pmxT = b.sb("l3pT", [32, HALF], F32)
        zpm = b.sb("l3zpm", [32, HALF], BF16)
        xrow = b.sb("l3xr", [128, E], F32)
        srow = b.sb("l3sr", [128, E], F32)
        srow2 = b.sb("l3sr2", [128, E], F32)
        hrow = b.sb("l3hr", [128, E], F32)
        ostg = b.sb("l3os", [128, 512], F32)
        pA = b.ps("l3pA", [128, 512], F32)
        pB = b.ps("l3pB", [128, 512], F32)

        @b.block.sync
        def _(sync):
            b.wge(sync, "cast")
            b.inc(sync.dma_start(out=pmx[:], in_=pmall), "ld", 16)
            b.inc(sync.dma_start(out=limsb[:], in_=limesB), "ld", 16)
            b.inc(sync.dma_start(out=bsb[:], in_=BmatB), "ld", 16)
            b.inc(sync.dma_start(out=zsb[:], in_=zTo), "ld", 16)

        @b.block.vector
        def _(vector):
            b.wge(vector, "ld")
            b.inc(vector.reduce_max(scl[:, 0:1], pmx[:], axis=AX), "dve")
            b.wge(vector, "dve")
            b.inc(vector.tensor_scalar_max(scl[:, 1:2], scl[:, 0:1], 1e-6), "dve")
            b.wge(vector, "dve")
            b.inc(vector.reciprocal(scl[:, 2:3], scl[:, 1:2]), "dve")
            b.wge(vector, "dve")
            b.inc(vector.tensor_scalar_mul(scl[:, 3:4], scl[:, 2:3], 0.5), "dve")

        @b.block.gpsimd
        def _(gpsimd):
            b.wge(gpsimd, "dve")
            b.inc(gpsimd.dma_start(out=rH[:], in_=scl[0:1, 3:4]), "st", 16)
            b.inc(gpsimd.dma_start(out=rD[:], in_=scl[1:2, 3:4]), "st", 16)

        @b.block.tensor
        def _(tensor):
            b.wge(tensor, "st")
            b.inc(tensor.matmul(pA[:, 0:1], ones_rf[:], rH[:], start=True, stop=True), "pe")
            b.inc(tensor.matmul(pA[:, 1:2], ones_rf[:], rD[:], start=True, stop=True), "pe")

        @b.block.vector
        def _(vector):
            b.wge(vector, "pe")
            b.inc(vector.tensor_copy(rHc[:], pA[:, 0:1]), "dve")
            b.inc(vector.tensor_copy(rDc[:], pA[:, 1:2]), "dve")
        b.barrier()

        for at in range(1024 // 128):
            @b.block.sync
            def _(sync, at=at):
                b.wge(sync, "pe")
                b.wge(sync, "dve")
                b.inc(sync.dma_start(out=hs[:], in_=Hso[at * 128:(at + 1) * 128, :]), "ld", 16)
                b.inc(sync.dma_start(out=dss[:], in_=dso[at * 128:(at + 1) * 128, :]), "ld", 16)

            @b.block.vector
            def _(vector):
                b.wge(vector, "ld")
                b.inc(vector.tensor_scalar(out=lg[:], in0=hs[:], scalar1=rHc[:, 0:1],
                                           scalar2=None, op0=ALU.mult), "dve")
                b.inc(vector.tensor_scalar(out=red[:], in0=dss[:], scalar1=rDc[:, 0:1],
                                           scalar2=None, op0=ALU.mult), "dve")
                b.wge(vector, "dve")
                b.inc(vector.tensor_add(lg[:], lg[:], red[:]), "dve")
                b.wge(vector, "dve")
                b.inc(vector.reduce_max(red[:, 0:1], lg[:], axis=AX), "dve")

            @b.block.scalar
            def _(scalar):
                b.wge(scalar, "dve")
                b.inc(scalar.activation(out=red[:, 1:2], in_=red[:, 0:1], func=AF.Copy,
                                        scale=-1.0), "act")

            @b.block.scalar
            def _(scalar):
                b.wge(scalar, "act")
                b.inc(scalar.activation(out=pr[:], in_=lg[:], func=AF.Exp,
                                        bias=red[:, 1:2], accum_out=red[:, 2:3]), "act")

            @b.block.vector
            def _(vector):
                b.wge(vector, "act")
                b.inc(vector.reciprocal(red[:, 3:4], red[:, 2:3]), "dve")
                b.wge(vector, "dve")
                b.inc(vector.tensor_scalar(out=pr[:], in0=pr[:], scalar1=red[:, 3:4],
                                           scalar2=None, op0=ALU.mult), "dve")
                b.wge(vector, "dve")
                b.inc(vector.reduce_max(red[:, 4:5], pr[:], axis=AX), "dve")
                b.wge(vector, "dve")
                b.inc(vector.tensor_scalar(out=msk[:], in0=pr[:], scalar1=red[:, 4:5],
                                           scalar2=None, op0=ALU.is_ge), "dve")
                b.wge(vector, "dve")
                b.inc(vector.reduce_sum(red[:, 5:6], msk[:], axis=AX), "dve")
                b.inc(vector.scalar_tensor_tensor(out=red[:, 6:7] if False else lg[:],
                                                  in0=msk[:], scalar=-1e30, in1=pr[:],
                                                  op0=ALU.mult, op1=ALU.add), "dve")
                b.wge(vector, "dve")
                b.inc(vector.reduce_max(red[:, 6:7], lg[:], axis=AX), "dve")
                b.wge(vector, "dve")
                b.inc(vector.tensor_scalar(out=red[:, 7:8], in0=red[:, 5:6], scalar1=2.0,
                                           scalar2=None, op0=ALU.is_ge), "dve")
                b.wge(vector, "dve")
                b.inc(vector.tensor_sub(red[:, 5:6], red[:, 4:5], red[:, 6:7]), "dve")
                b.wge(vector, "dve")
                b.inc(vector.tensor_mul(red[:, 5:6], red[:, 5:6], red[:, 7:8]), "dve")
                b.wge(vector, "dve")
                b.inc(vector.tensor_add(red[:, 6:7], red[:, 6:7], red[:, 5:6]), "dve")
                b.wge(vector, "dve")
                b.inc(vector.tensor_scalar_mul(red[:, 7:8], red[:, 6:7], -2.0), "dve")

            @b.block.scalar
            def _(scalar):
                b.wge(scalar, "dve")
                b.inc(scalar.activation(out=msk[:], in_=pr[:], func=AF.Sigmoid,
                                        scale=2.0, bias=red[:, 7:8]), "act")

            @b.block.vector
            def _(vector):
                b.wge(vector, "act")
                b.inc(vector.tensor_mul(msk[:], msk[:], pr[:]), "dve")
                b.wge(vector, "dve")
                b.inc(vector.reduce_sum(red[:, 0:1], msk[:], axis=AX), "dve")
                b.wge(vector, "dve")
                b.inc(vector.tensor_scalar_add(red[:, 0:1], red[:, 0:1], 1e-9), "dve")
                b.wge(vector, "dve")
                b.inc(vector.reciprocal(red[:, 1:2], red[:, 0:1]), "dve")
                b.wge(vector, "dve")
                b.inc(vector.tensor_scalar(out=wf[:], in0=msk[:], scalar1=red[:, 1:2],
                                           scalar2=None, op0=ALU.mult), "dve")

            @b.block.tensor
            def _(tensor, at=at):
                b.wge(tensor, "dve")
                b.inc(tensor.matmul(pB[0:8, at % 4 * 128:(at % 4 + 1) * 128],
                                    wf[:], ident[:], start=True, stop=True), "pe")

            if at % 4 == 3:
                @b.block.vector
                def _(vector, at=at):
                    b.wge(vector, "pe")
                    b.inc(vector.tensor_copy(wfT[0:8, (at - 3) * 128:(at + 1) * 128],
                                             pB[0:8, 0:512]), "dve")
        b.barrier()

        @b.block.vector
        def _(vector):
            b.inc(vector.tensor_copy(wfull[:, 0:HALF:2], wfT[:]), "dve")
            b.inc(vector.tensor_copy(wfull[:, 1:HALF:2], wfT[:]), "dve")
        b.barrier()

        for tgo in range(HALF // 512):
            @b.block.tensor
            def _(tensor, tgo=tgo):
                b.wge(tensor, "dve")
                b.wge(tensor, "act")
                b.inc(tensor.matmul(pA[0:32, :], limsb[:],
                                    wfull[:, tgo * 512:(tgo + 1) * 512],
                                    start=True, stop=True), "pe")

            @b.block.vector
            def _(vector, tgo=tgo):
                b.wge(vector, "pe")
                b.inc(vector.tensor_copy(pmxT[0:32, tgo * 512:(tgo + 1) * 512],
                                         pA[0:32, :]), "dve")
        b.barrier()

        @b.block.vector
        def _(vector):
            b.inc(vector.scalar_tensor_tensor(out=zpm[0:32, :], in0=zsb[0:32, :],
                                              scalar=0.5, in1=pmxT[0:32, :],
                                              op0=ALU.mult, op1=ALU.mult), "dve")
        b.barrier()

        for tt in range(HALF // 128):
            @b.block.sync
            def _(sync, tt=tt):
                b.wge(sync, "dve")
                b.inc(sync.dma_start(out=xrow[:], in_=x1o[tt * 128:(tt + 1) * 128, :]), "ld", 16)
            base_ld = b.n("ld")

            @b.block.gpsimd
            def _(gpsimd, tt=tt, base_ld=base_ld):
                b.wge(gpsimd, "ld", base_ld)
                for srcb in (S1a, S1b, Hin):
                    b.inc(gpsimd.dma_start(out=xrow[:, 0:2048],
                                           in_=srcb[tt * 128:(tt + 1) * 128, 0:2048],
                                           accum_op=ALU.add), "acc", 16)
                    b.inc(gpsimd.dma_start(out=xrow[:, 2048:4096],
                                           in_=srcb[tt * 128:(tt + 1) * 128, 2048:4096],
                                           accum_op=ALU.add), "acc", 16)
                    if srcb is not Hin:
                        b.wge(gpsimd, "acc")

            @b.block.vector
            def _(vector):
                b.wge(vector, "acc")

            for eg in range(8):
                @b.block.tensor
                def _(tensor, tt=tt, eg=eg):
                    b.wge(tensor, "dve")
                    b.wge(tensor, "act")
                    b.inc(tensor.matmul(pB[:], zpm[0:32, tt * 128:(tt + 1) * 128],
                                        bsb[0:32, eg * 512:(eg + 1) * 512],
                                        start=True, stop=True), "pe")

                @b.block.vector
                def _(vector, eg=eg):
                    b.wge(vector, "pe")
                    b.inc(vector.tensor_add(ostg[:], pB[:], xrow[:, eg * 512:(eg + 1) * 512]), "dve")

                @b.block.scalar
                def _(scalar, tt=tt, eg=eg):
                    b.wge(scalar, "dve")
                    b.inc(scalar.dma_start(out=out[tt * 128:(tt + 1) * 128,
                                                   eg * 512:(eg + 1) * 512],
                                           in_=ostg[:]), "st", 16)
                @b.block.vector
                def _(vector):
                    b.wge(vector, "st")

        @b.block.sync
        def _(sync):
            b.wge(sync, "st")
    return b.nc


_TABLES = None


def _tables():
    global _TABLES
    if _TABLES is None:
        _TABLES = _host_tables()
    return _TABLES


def _with_consts(m, names):
    tabs = _tables()
    for nm in names:
        m[nm] = tabs[nm]
    return m


_L1_CONSTS = ("ident128", "ones_col", "ones_row_f32", "cmask",
              "rp0cq", "rp0sq", "rp0ck", "rp0sk")
_L2_CONSTS = ("ident128", "ident8", "ones_col", "ones_row_f32", "ones_row_bf",
              "alpha_row", "rp1cq", "rp1sq", "rp1ck", "rp1sk")
_L3_CONSTS = ("ident128", "ident8", "ones_col", "ones_row_f32", "ones_row_bf")


def _l1_inmaps(inputs):
    maps = []
    for c in range(NCORES):
        p, h = c // 2, c % 2
        x = np.ascontiguousarray(np.asarray(inputs["L0_attn_x"][p]))
        m = {
            "x_full": x,
            "x_own": np.ascontiguousarray(x[h * HALF:(h + 1) * HALF]),
            "wq": np.ascontiguousarray(np.asarray(inputs["L0_attn_Wq"])[h * NH * DH:(h + 1) * NH * DH]),
            "wk": np.ascontiguousarray(np.asarray(inputs["L0_attn_Wk"])[h * NKV0 * DH:(h + 1) * NKV0 * DH]),
            "wv": np.ascontiguousarray(np.asarray(inputs["L0_attn_Wv"])[h * NKV0 * DH:(h + 1) * NKV0 * DH]),
            "wo": np.ascontiguousarray(np.asarray(inputs["L0_attn_Wo"])[:, h * NH * DH:(h + 1) * NH * DH]),
            "fw": np.asarray(inputs["L0_ffn_weight"]),
            "fws": np.asarray(inputs["L0_ffn_wscale"]),
            "fwb": np.asarray(inputs["L0_ffn_wbias"]),
        }
        maps.append(_with_consts(m, _L1_CONSTS))
    return maps


def _l2_inmaps(inputs, r1):
    maps = []
    for c in range(NCORES):
        p, h = c // 2, c % 2
        ca, cb = 2 * p, 2 * p + 1
        x = np.ascontiguousarray(np.asarray(inputs["L0_attn_x"][p]))
        S0 = np.asarray(r1[ca]["S_h"])
        S1 = np.asarray(r1[cb]["S_h"])
        Zfc = np.concatenate([np.asarray(r1[ca]["Zf"]), np.asarray(r1[cb]["Zf"])], axis=0)
        own = slice(h * HALF, (h + 1) * HALF)
        m = {
            "x_full": x, "S0": S0, "S1i": S1, "Zfc": Zfc,
            "x_own": np.ascontiguousarray(x[own]),
            "S0o": np.ascontiguousarray(S0[own]),
            "S1o": np.ascontiguousarray(S1[own]),
            "Zfo": np.ascontiguousarray(Zfc[own]),
            "wq1": np.ascontiguousarray(np.asarray(inputs["L1_attn_Wq"])[h * NH * DH:(h + 1) * NH * DH]),
            "wk1": np.asarray(inputs["L1_attn_Wk"]),
            "wv1": np.asarray(inputs["L1_attn_Wv"]),
            "wo1": np.ascontiguousarray(np.asarray(inputs["L1_attn_Wo"])[:, h * NH * DH:(h + 1) * NH * DH]),
            "Wf": np.asarray(inputs["L1_ffn_W"]),
            "bvec": np.asarray(inputs["L1_ffn_b"]).reshape(1, E),
            "Amat": np.asarray(inputs["L1_ffn_A"]),
            "W8": np.ascontiguousarray(np.asarray(inputs["L1_ffn_W"])[0:8]),
            "limes": np.asarray(inputs["L1_ffn_LiMEs"]),
        }
        maps.append(_with_consts(m, _L2_CONSTS))
    return maps


def _l3_inmaps(inputs, r2):
    pm = np.stack([np.asarray(r2[c]["pmax"]) for c in range(NCORES)])
    pmall = np.ascontiguousarray(pm.transpose(2, 0, 1).reshape(2, NCORES * 128)[:, 0:1024])
    maps = []
    for c in range(NCORES):
        p, h = c // 2, c % 2
        ca, cb = 2 * p, 2 * p + 1
        own = slice(h * HALF, (h + 1) * HALF)
        owna = slice(h * 1024, (h + 1) * 1024)
        m = {
            "x1o": np.asarray(r2[c]["x1_own"]),
            "S1a": np.ascontiguousarray(np.asarray(r2[ca]["S1_h"])[own]),
            "S1b": np.ascontiguousarray(np.asarray(r2[cb]["S1_h"])[own]),
            "Hin": np.asarray(r2[c]["Hout"]),
            "zTo": np.ascontiguousarray(np.asarray(r2[c]["zT"])[:, own]),
            "Hso": np.ascontiguousarray(np.asarray(r2[c]["Hs"])[owna]),
            "dso": np.ascontiguousarray(np.asarray(r2[c]["ds"])[owna]),
            "pmall": pmall,
            "Bmat": np.asarray(inputs["L1_ffn_Bmat"]),
            "limes": np.asarray(inputs["L1_ffn_LiMEs"]),
        }
        maps.append(_with_consts(m, _L3_CONSTS))
    return maps


_NC_CACHE = {}


def _get_nc(which):
    if which not in _NC_CACHE:
        _NC_CACHE[which] = {"l1": build_launch1, "l2": build_launch2,
                            "l3": build_launch3}[which]()
    return _NC_CACHE[which]


def kernel(**inputs):
    cores = list(range(NCORES))
    r1 = run_bass_kernel_spmd(_get_nc("l1"), _l1_inmaps(inputs), core_ids=cores).results
    r2 = run_bass_kernel_spmd(_get_nc("l2"), _l2_inmaps(inputs, r1), core_ids=cores).results
    r3 = run_bass_kernel_spmd(_get_nc("l3"), _l3_inmaps(inputs, r2), core_ids=cores).results
    out = np.empty((4, T, E), np.float32)
    for c in range(NCORES):
        p, h = c // 2, c % 2
        out[p, h * HALF:(h + 1) * HALF] = np.asarray(r3[c]["out"])
    return out
